# version 01
# baseline (speedup 1.0000x reference)
import numpy as np
import ml_dtypes
import concourse.bass as bass
import concourse.mybir as mybir
from concourse.bass_utils import run_bass_kernel_spmd
from contextlib import ExitStack

F32 = mybir.dt.float32
BF16 = mybir.dt.bfloat16
AF = mybir.ActivationFunctionType
ALU = mybir.AluOpType
AX = mybir.AxisListType
bfdt = ml_dtypes.bfloat16

P = 128
D = 1024
NCH = 8
TT = 512
NTILE = 8
SEQ = 2048
B = 2
HEADS = 16
DK = 64
DEPTH = 6
W1O = 5460
HALF1 = 2730
NTOK = B * SEQ
EXP_BIAS = -10.0

def chunks_of(total, step=128):
    out = []
    off = 0
    while off < total:
        out.append((off, min(step, total - off)))
        off += step
    return out

W2CH = chunks_of(HALF1)


def host_constants():
    c = {}
    c["ident_f32"] = np.eye(128, dtype=np.float32)
    c["negI_bf"] = (-1e30 * np.eye(128)).astype(bfdt)
    kk = np.arange(128)[:, None]
    ii = np.arange(128)[None, :]
    m0 = (kk > ii).astype(np.float32)
    m2 = (kk < ii).astype(np.float32)
    c["maskU_bf"] = np.concatenate([m0, np.zeros((128, 128), np.float32), m2], 1).astype(bfdt)
    sub_k = kk // 16
    sub_q = ii // 16
    mt = ((sub_k < sub_q) | ((sub_k == sub_q) & (kk <= ii))).astype(np.float32)
    c["maskTT_bf"] = mt.astype(bfdt)
    c["ones_col_bf"] = np.ones((128, 1), dtype=bfdt)
    c["ones_row_bf"] = np.ones((1, 128), dtype=bfdt)
    c["ones_row_f32"] = np.ones((1, 128), dtype=np.float32)
    sel2 = np.zeros((128, 2), np.float32)
    sel2[:64, 0] = 1.0
    sel2[64:, 1] = 1.0
    c["sel2_bf"] = sel2.astype(bfdt)
    c["selT_bf"] = sel2.T.copy().astype(bfdt)
    r = np.zeros((64, 64), np.float32)
    for d in range(32):
        r[d + 32, d] = -1.0
        r[d, d + 32] = 1.0
    R = np.zeros((128, 128), np.float32)
    R[:64, :64] = r
    R[64:, 64:] = r
    c["R2T_bf"] = R.astype(bfdt)
    half = np.arange(0, DK, 2, dtype=np.float64)
    inv_freq = 1.0 / 10000.0 ** (half / DK)
    t = np.arange(SEQ, dtype=np.float64)
    fr32 = t[:, None] * inv_freq[None, :]
    freqs = np.concatenate([fr32, fr32], 1)
    base_scale = (half + 0.4 * DK) / (1.4 * DK)
    power = (t - SEQ // 2) / (256 // 2)
    sc32 = base_scale[None, :] ** power[:, None]
    scale = np.concatenate([sc32, sc32], 1)
    cos = np.cos(freqs)
    sin = np.sin(freqs)
    A = (cos * scale).T
    Bt = (sin * scale).T
    C = (cos / scale).T
    Dt = (sin / scale).T
    dup = lambda m: np.concatenate([m, m], 0).astype(bfdt)
    c["rotA"] = dup(A)
    c["rotB"] = dup(Bt)
    c["rotC"] = dup(C)
    c["rotD"] = dup(Dt)
    cv = np.zeros((128, 8), np.float32)
    cv[:, 0] = 1e-5
    cv[:, 1] = 1e-12
    cv[:, 2] = EXP_BIAS
    cv[:, 3] = 1.0 / 64.0
    cv[:, 4] = 1.0 / 1024.0
    c["cvec"] = cv
    return c

CONST_SPECS = {
    "ident_f32": F32, "negI_bf": BF16, "maskU_bf": BF16, "maskTT_bf": BF16,
    "ones_col_bf": BF16, "ones_row_bf": BF16, "ones_row_f32": F32,
    "sel2_bf": BF16, "selT_bf": BF16, "R2T_bf": BF16,
    "rotA": BF16, "rotB": BF16, "rotC": BF16, "rotD": BF16, "cvec": F32,
}

DBG_SHAPES = {"XN32": ([D, NTOK], F32), "XNBF": ([D, NTOK], BF16),
              "S32": ([D, NTOK], F32), "XA32": ([D, NTOK], F32),
              "XB32": ([D, NTOK], F32), "XBBF": ([D, NTOK], BF16),
              "QD": ([D, NTOK], BF16), "KD": ([D, NTOK], BF16),
              "VD": ([D, NTOK], BF16), "GD": ([D, NTOK], BF16),
              "OD": ([NTOK, D], BF16),
              "Q2D": ([D, NTOK], BF16), "K2D": ([D, NTOK], BF16),
              "V2D": ([D, NTOK], BF16), "OGD": ([D, NTOK], BF16),
              "WQKVB": ([DEPTH * D, 3 * D], BF16), "W1B": ([DEPTH * D, W1O], BF16)}



class KB:

    def __init__(self, nc):
        self.nc = nc
        self.sems = {}
        self.cnt = {}
        self.waited = {}

    def bump(self, instr, sem_name, k=1):
        instr.then_inc(self.sems[sem_name], k)
        self.cnt[sem_name] += k
        return self.cnt[sem_name]

    def val(self, sem_name):
        return self.cnt[sem_name]

    def fence(self, eng, eng_name, instr, sem_name):
        v = self.bump(instr, sem_name, 1)
        self.wait(eng, eng_name, sem_name, v)

    DMA_SEMS = ("ld", "wt", "st", "tr")

    def wait(self, eng, eng_name, sem_name, v=None):
        if v is None or sem_name in KB.DMA_SEMS:
            v = self.cnt[sem_name]
        if v <= 0:
            return
        key = (eng_name, sem_name)
        if self.waited.get(key, 0) >= v:
            return
        eng.wait_ge(self.sems[sem_name], v)
        self.waited[key] = v


def build_program(debug_names=(), phases=("p0", "p1", "p2", "p3"), depth=DEPTH, p2_sub=None, paranoid=False):
    debug_names = set(debug_names)
    nc = bass.Bass("TRN2", target_bir_lowering=False)
    consts = host_constants()

    xin = nc.dram_tensor("x", [NTOK, D], F32, kind="ExternalInput").ap()
    out_ap = nc.dram_tensor("out", [NTOK, D], F32, kind="ExternalOutput").ap()
    wt = {}
    for nm, shape in [
        ("aWq", [D, D]), ("aWk", [D, D]), ("aWv", [D, D]), ("aWg", [D, D]), ("aWo", [D, D]),
        ("acq", [D, 4]), ("ack", [D, 4]), ("acv", [D, 4]),
        ("aW0", [16 * DK, DK]), ("agam", [DK]), ("abet", [DK]),
        ("fWqkv", [DEPTH, D, 3 * D]), ("fqs", [DEPTH, DK]), ("fks", [DEPTH, DK]),
        ("fWo", [DEPTH, D, D]), ("flng", [DEPTH, D]), ("flnb", [DEPTH, D]),
        ("fW1", [DEPTH, D, W1O]), ("fb1", [DEPTH, W1O]), ("fW2", [DEPTH, HALF1, D]),
        ("fb2", [DEPTH, D]),
    ]:
        wt[nm] = nc.dram_tensor(nm, shape, F32, kind="ExternalInput").ap()
    cst_d = {}
    for nm, dt in CONST_SPECS.items():
        cst_d[nm] = nc.dram_tensor("c_" + nm, list(consts[nm].shape), dt, kind="ExternalInput").ap()

    hb = {}
    for nm, dt in [("XN32", F32), ("XNBF", BF16), ("XA32", F32), ("XABF", BF16),
                   ("XB32", F32), ("XBBF", BF16), ("S32", F32),
                   ("QD", BF16), ("KD", BF16), ("VD", BF16), ("GD", BF16)]:
        hb[nm] = nc.dram_tensor(nm, [D, NTOK], dt, kind="Internal").ap()
    hb["OD"] = nc.dram_tensor("OD", [NTOK, D], BF16, kind="Internal").ap()
    for nm in ["Q2D", "K2D", "V2D", "OGD"]:
        hb[nm] = nc.dram_tensor(nm, [D, NTOK], BF16, kind="Internal").ap()
    hb["WQKVB"] = nc.dram_tensor("WQKVB", [DEPTH * D, 3 * D], BF16, kind="Internal").ap()
    hb["WOB"] = nc.dram_tensor("WOB", [DEPTH * D, D], BF16, kind="Internal").ap()
    hb["W1B"] = nc.dram_tensor("W1B", [DEPTH * D, W1O], BF16, kind="Internal").ap()
    hb["W2B"] = nc.dram_tensor("W2B", [DEPTH * HALF1, D], BF16, kind="Internal").ap()

    dbg = {}
    for dn in debug_names:
        shp, dt = DBG_SHAPES[dn]
        dbg[dn] = nc.dram_tensor("dbg_" + dn, shp, dt, kind="ExternalOutput").ap()

    kb = KB(nc)

    with ExitStack() as ctx:
        sbt = lambda name, shape, dt: ctx.enter_context(nc.sbuf_tensor(name, shape, dt))
        for s in ["ld", "wt", "tr", "st", "pe", "dve", "act", "pool"]:
            kb.sems[s] = ctx.enter_context(nc.semaphore("s_" + s))
            kb.cnt[s] = 0

        sbc = {}
        for nm, dt in CONST_SPECS.items():
            sbc[nm] = sbt("k_" + nm, list(consts[nm].shape), dt)
        gam_bc = sbt("gam_bc", [128, 64], F32)
        bet_bc = sbt("bet_bc", [128, 64], F32)
        convw = sbt("convw", [128, 3 * 32], F32)
        convp3 = sbt("convp3", [128, 3 * 8], F32)
        W0_sb = sbt("W0_sb", [128, 8 * 64], F32)
        gb_row = sbt("gb_row", [1, 128], F32)
        qs_col = sbt("qs_col", [128, 1], F32)
        ks_col = sbt("ks_col", [128, 1], F32)
        b1_cols = sbt("b1_cols", [128, 44], F32)
        b2_cols = sbt("b2_cols", [128, 8], F32)
        lng_cols = sbt("lng_cols", [128, 8], F32)
        lnb_cols = sbt("lnb_cols", [128, 8], F32)

        pb = [ctx.enter_context(nc.psum_tensor(f"pb{i}", [128, 512], F32)) for i in range(8)]

        block = ctx.enter_context(nc.Block())

        def sec(eng_name):
            raw = getattr(block, eng_name)
            pset = set(paranoid.split(",")) if isinstance(paranoid, str) and paranoid else (
                {"sync", "tensor", "vector", "scalar", "gpsimd"} if paranoid is True else set())
            if eng_name not in pset:
                return raw

            def deco(fn):
                def wrapped(eng):
                    for s in ["pe", "dve", "act", "pool", "ld", "wt", "st", "tr"]:
                        kb.wait(eng, eng_name, s)
                    return fn(eng)
                return raw(wrapped)
            return deco

        CV = sbc["cvec"]

        @sec("sync")
        def _(sync):
            for nm in CONST_SPECS:
                kb.bump(sync.dma_start(out=sbc[nm][:], in_=cst_d[nm]), "ld", 16)
            for pi, nm in enumerate(["acq", "ack", "acv"]):
                for cch in range(NCH):
                    kb.bump(sync.dma_start(
                        out=convw[:, pi * 32 + 4 * cch: pi * 32 + 4 * cch + 4],
                        in_=wt[nm][cch * 128:(cch + 1) * 128, :]), "ld", 16)
            for cch in range(NCH):
                kb.bump(sync.dma_start(
                    out=W0_sb[:, cch * 64:(cch + 1) * 64],
                    in_=wt["aW0"][cch * 128:(cch + 1) * 128, :]), "ld", 16)
            kb.bump(sync.dma_start(out=gb_row[0:1, 0:64], in_=wt["agam"][None, :]), "ld", 16)
            kb.bump(sync.dma_start(out=gb_row[0:1, 64:128], in_=wt["abet"][None, :]), "ld", 16)

        @sec("vector")
        def _(vector):
            kb.wait(vector, "vector", "ld")
            for pi in range(3):
                src = convw[:, pi * 32 + 3: (pi + 1) * 32: 4]
                kb.bump(vector.tensor_scalar(
                    convp3[:, pi * 8:(pi + 1) * 8], src, 1.0, None, ALU.add), "dve", 1)

        @sec("tensor")
        def _(tensor):
            kb.wait(tensor, "tensor", "ld")
            kb.bump(tensor.matmul(pb[0][:, 0:64], sbc["ones_row_f32"][0:1, :],
                                  gb_row[0:1, 0:64], start=True, stop=True), "pe", 1)
            kb.bump(tensor.matmul(pb[1][:, 0:64], sbc["ones_row_f32"][0:1, :],
                                  gb_row[0:1, 64:128], start=True, stop=True), "pe", 1)

        @sec("scalar")
        def _(scalar):
            kb.wait(scalar, "scalar", "pe")
            scalar.copy(gam_bc[:], pb[0][:, 0:64])
            kb.bump(scalar.copy(bet_bc[:], pb[1][:, 0:64]), "act", 1)

        if "p0" in phases:
            with ExitStack() as pctx:
                sb0 = lambda name, shape, dt: pctx.enter_context(nc.sbuf_tensor(name, shape, dt))
                xt = [sb0(f"p0_xt{i}", [128, 1024], F32) for i in range(2)]
                stat = sb0("p0_stat", [128, 16], F32)
                xn_f = [sb0(f"p0_xn{i}", [128, 1024], F32) for i in range(2)]
                xnT_f = [sb0(f"p0_xnT{i}", [128, 1024], F32) for i in range(2)]
                xnT_b = [sb0(f"p0_xnTb{i}", [128, 1024], BF16) for i in range(2)]

                NT0 = NTOK // 128
                dve_done = [0] * NT0
                pe_done = [0] * NT0
                ev_done = [0] * NT0
                st_done = [0] * NT0
                init_act = kb.val("act")

                for i in range(NT0):
                    @sec("sync")
                    def _(sync, i=i):
                        if i >= 2:
                            kb.wait(sync, "sync", "dve", dve_done[i - 2])
                        kb.bump(sync.dma_start(out=xt[i % 2][:],
                                               in_=xin[i * 128:(i + 1) * 128, :]), "ld", 16)

                    @sec("vector")
                    def _(vector, i=i):
                        kb.wait(vector, "vector", "ld")
                        t = xt[i % 2]
                        vector.bn_stats(stat[:, 4:10], t[:, 0:512])
                        kb.fence(vector, "vector",
                                 vector.bn_stats(stat[:, 10:16], t[:, 512:1024]), "dve")
                        kb.bump(vector.bn_aggr(stat[:, 0:2], stat[:, 4:16]), "dve", 1)

                    @sec("scalar")
                    def _(scalar, i=i):
                        kb.wait(scalar, "scalar", "dve")
                        kb.bump(scalar.activation(stat[:, 2:3], stat[:, 1:2], AF.Sqrt,
                                                  bias=CV[:, 0:1]), "act", 1)

                    @sec("vector")
                    def _(vector, i=i):
                        kb.wait(vector, "vector", "act")
                        kb.fence(vector, "vector",
                                 vector.reciprocal(stat[:, 3:4], stat[:, 2:3]), "dve")
                        if i >= 2:
                            kb.wait(vector, "vector", "pe", pe_done[i - 2])
                        dve_done[i] = kb.bump(
                            vector.tensor_scalar(xn_f[i % 2][:], xt[i % 2][:], stat[:, 0:1],
                                                 stat[:, 3:4], ALU.subtract, ALU.mult), "dve", 1)

                    @sec("tensor")
                    def _(tensor, i=i):
                        kb.wait(tensor, "tensor", "dve", dve_done[i])
                        kb.wait(tensor, "tensor", "act",
                                ev_done[i - 1] if i >= 1 else init_act)
                        for cch in range(8):
                            ins = tensor.transpose(pb[cch][:, 0:128],
                                                   xn_f[i % 2][:, cch * 128:(cch + 1) * 128],
                                                   sbc["ident_f32"][:])
                        pe_done[i] = kb.bump(ins, "pe", 1)

                    @sec("scalar")
                    def _(scalar, i=i):
                        kb.wait(scalar, "scalar", "pe", pe_done[i])
                        if i >= 2:
                            kb.wait(scalar, "scalar", "st", st_done[i - 2])
                        for cch in range(8):
                            scalar.copy(xnT_f[i % 2][:, cch * 128:(cch + 1) * 128],
                                        pb[cch][:, 0:128])
                            ins = scalar.copy(xnT_b[i % 2][:, cch * 128:(cch + 1) * 128],
                                              pb[cch][:, 0:128])
                        ev_done[i] = kb.bump(ins, "act", 1)

                    @sec("sync")
                    def _(sync, i=i):
                        kb.wait(sync, "sync", "act", ev_done[i])
                        kb.bump(sync.dma_start(
                            out=hb["XN32"].rearrange("(c p) t -> p c t", p=128)[:, :, i * 128:(i + 1) * 128],
                            in_=xnT_f[i % 2][:].rearrange("p (c t) -> p c t", c=8)), "st", 16)
                        st_done[i] = kb.bump(sync.dma_start(
                            out=hb["XNBF"].rearrange("(c p) t -> p c t", p=128)[:, :, i * 128:(i + 1) * 128],
                            in_=xnT_b[i % 2][:].rearrange("p (c t) -> p c t", c=8)), "st", 16)

        if "p1" in phases:
            with ExitStack() as pctx:
                sb1 = lambda name, shape, dt: pctx.enter_context(nc.sbuf_tensor(name, shape, dt))
                WT = {w: sb1(f"p1_WT_{w}", [128, 8 * 1024], BF16) for w in "qkvgo"}
                with ExitStack() as wctx:
                    stage_f = [wctx.enter_context(nc.sbuf_tensor(f"p1_ws{i}", [128, 1024], F32))
                               for i in range(2)]
                    stage_b = [wctx.enter_context(nc.sbuf_tensor(f"p1_wb{i}", [128, 1024], BF16))
                               for i in range(2)]
                    cast_mark = {}
                    tr_mark = {}
                    for wi, (w, src_nm) in enumerate(zip("qkvgo", ["aWq", "aWk", "aWv", "aWg", "aWo"])):
                        for r in range(8):
                            n = wi * 8 + r
                            @sec("sync")
                            def _(sync, src_nm=src_nm, r=r, n=n):
                                if n == 0:
                                    kb.wait(sync, "sync", "wt")
                                    kb.wait(sync, "sync", "pool")
                                if n >= 2:
                                    kb.wait(sync, "sync", "pool", cast_mark[n - 2])
                                kb.bump(sync.dma_start(out=stage_f[n % 2][:],
                                                       in_=wt[src_nm][r * 128:(r + 1) * 128, :]),
                                        "wt", 16)

                            @sec("gpsimd")
                            def _(gpsimd, n=n):
                                kb.wait(gpsimd, "gpsimd", "wt")
                                if n >= 2:
                                    kb.wait(gpsimd, "gpsimd", "tr", tr_mark[n - 2])
                                cast_mark[n] = kb.bump(
                                    gpsimd.tensor_copy(stage_b[n % 2][:], stage_f[n % 2][:]),
                                    "pool", 1)

                            @sec("scalar")
                            def _(scalar, w=w, r=r, n=n):
                                kb.wait(scalar, "scalar", "pool", cast_mark[n])
                                for j in range(8):
                                    scalar.dma_start_transpose(
                                        WT[w][:, j * 1024 + r * 128: j * 1024 + (r + 1) * 128],
                                        stage_b[n % 2][:, j * 128:(j + 1) * 128]).then_inc(kb.sems["tr"], 16)
                                kb.cnt["tr"] += 16 * 8
                                tr_mark[n] = kb.val("tr")

                xnt = sb1("p1_xnt", [128, 8 * TT], BF16)
                q_bf = sb1("p1_q", [128, 8 * TT], BF16)
                k_bf = sb1("p1_k", [128, 8 * TT], BF16)
                v_bf = sb1("p1_v", [128, 8 * TT], BF16)
                g_bf = sb1("p1_g", [128, 8 * TT], BF16)
                u_ext = sb1("p1_uext", [128, 515], F32)
                ca_b = sb1("p1_ca", [128, 512], F32)
                cb_b = sb1("p1_cb", [128, 512], F32)
                qpre = sb1("p1_qpre", [128, 512], BF16)
                sqv = sb1("p1_sqv", [128, 512], BF16)
                sd_row = sb1("p1_sd", [2, 512], F32)
                rstd_bf = sb1("p1_rstd", [2, 512], BF16)
                tails = sb1("p1_tails", [128, 72], F32)
                ktok = [sb1(f"p1_ktok{i}", [128, 1024], BF16) for i in range(2)]
                vtok = [sb1(f"p1_vtok{i}", [128, 1024], BF16) for i in range(2)]
                gtok = [sb1(f"p1_gtok{i}", [128, 1024], BF16) for i in range(2)]
                W_bf = sb1("p1_Wbf", [128, 512], BF16)
                ptT = [sb1(f"p1_ptT{i}", [128, 128], BF16) for i in range(2)]
                mvstat = sb1("p1_mvstat", [128, 96], F32)
                mv = sb1("p1_mv", [128, 32], F32)
                rinv = sb1("p1_rinv", [128, 16], F32)
                sdv = sb1("p1_sdv", [128, 16], F32)
                t2_f = sb1("p1_t2", [128, 1024], F32)
                t_f = sb1("p1_t", [128, 1024], F32)
                o_all = sb1("p1_oall", [128, 4 * 1024], BF16)
                og = sb1("p1_og", [128, 4 * 1024], BF16)
                og_fm = sb1("p1_ogfm", [128, 8 * TT], BF16)
                s_stage = [sb1(f"p1_ss{i}", [128, 512], F32) for i in range(2)]

                mk = {"xnt_free": 0, "q_free": 0, "k_free": 0, "v_free": 0, "g_free": 0,
                      "bank_free": {2: ("pe", 0), 3: ("pe", 0), 4: ("pe", 0),
                                    5: ("dve", 0), 6: ("dve", 0), 7: ("act", 0)},
                      "uext_free": 0, "qpre_free": 0, "sqv_free": 0,
                      "wbf_evac": 0, "o_epi": {0: 0, 1: 0},
                      "tok_free": {0: 0, 1: 0}, "og_done": [0] * 4,
                      "sst_free": [0, 0]}

                rot_counter = [0]

                def B_chunk(pi, nm, oc, tile, first_of_seq):
                    bslot = 2 + (rot_counter[0] % 3)
                    rot_counter[0] += 1
                    bank = pb[bslot]

                    @sec("tensor")
                    def _(tensor):
                        kb.wait(tensor, "tensor", "ld", mk["xnt_ld"])
                        kb.wait(tensor, "tensor", "tr", mk["wt_ready"])
                        bs, bv = mk["bank_free"][bslot]
                        kb.wait(tensor, "tensor", bs, bv)
                        for kc in range(8):
                            ins = tensor.matmul(bank[:, 0:512],
                                                WT[nm][:, kc * 1024 + oc * 128: kc * 1024 + (oc + 1) * 128],
                                                xnt[:, kc * TT:(kc + 1) * TT],
                                                start=(kc == 0), stop=(kc == 7))
                        mk["proj_mm"] = kb.bump(ins, "pe", 1)

                    if nm == "g":
                        @sec("scalar")
                        def _(scalar):
                            kb.wait(scalar, "scalar", "pe", mk["proj_mm"])
                            kb.wait(scalar, "scalar", "tr", mk.get("tok_tr", 0))
                            v = kb.bump(scalar.copy(g_bf[:, oc * TT:(oc + 1) * TT], bank[:, 0:512]),
                                        "act", 1)
                            mk["bank_free"][bslot] = ("act", v)
                        return

                    @sec("scalar")
                    def _(scalar):
                        kb.wait(scalar, "scalar", "pe", mk["proj_mm"])
                        kb.wait(scalar, "scalar", "dve", mk["uext_free"])
                        v = kb.bump(scalar.copy(u_ext[:, 3:515], bank[:, 0:512]), "act", 1)
                        mk["uext_fill"] = v
                        mk["bank_free"][bslot] = ("act", v)

                    @sec("vector")
                    def _(vector):
                        kb.wait(vector, "vector", "act", mk["uext_fill"])
                        kb.wait(vector, "vector", "dve")
                        tcol = (pi * 8 + oc) * 3
                        if first_of_seq:
                            vector.memset(u_ext[:, 0:3], 0.0)
                        else:
                            vector.tensor_copy(u_ext[:, 0:3], tails[:, tcol:tcol + 3])
                        kb.fence(vector, "vector",
                                 vector.tensor_copy(tails[:, tcol:tcol + 3], u_ext[:, 512:515]),
                                 "dve")
                        wbase = pi * 32 + oc * 4
                        kb.fence(vector, "vector",
                                 vector.tensor_scalar(ca_b[:], u_ext[:, 3:515],
                                                      convp3[:, pi * 8 + oc: pi * 8 + oc + 1],
                                                      None, ALU.mult), "dve")
                        kb.fence(vector, "vector",
                                 vector.scalar_tensor_tensor(cb_b[:], u_ext[:, 2:514],
                                                             convw[:, wbase + 2: wbase + 3],
                                                             ca_b[:], ALU.mult, ALU.add), "dve")
                        kb.fence(vector, "vector",
                                 vector.scalar_tensor_tensor(ca_b[:], u_ext[:, 1:513],
                                                             convw[:, wbase + 1: wbase + 2],
                                                             cb_b[:], ALU.mult, ALU.add), "dve")
                        mk["conv_done"] = kb.bump(
                            vector.scalar_tensor_tensor(cb_b[:], u_ext[:, 0:512],
                                                        convw[:, wbase + 0: wbase + 1],
                                                        ca_b[:], ALU.mult, ALU.add), "dve", 1)
                        mk["uext_free"] = mk["conv_done"]

                    if nm == "v":
                        @sec("scalar")
                        def _(scalar):
                            kb.wait(scalar, "scalar", "dve", mk["conv_done"])
                            kb.wait(scalar, "scalar", "tr", mk.get("tok_tr", 0))
                            kb.bump(scalar.activation(v_bf[:, oc * TT:(oc + 1) * TT], cb_b[:],
                                                      AF.Silu), "act", 1)
                        return

                    dst = q_bf if nm == "q" else k_bf

                    @sec("scalar")
                    def _(scalar):
                        kb.wait(scalar, "scalar", "dve", mk["conv_done"])
                        kb.wait(scalar, "scalar", "dve", mk["qpre_free"])
                        mk["silu_done"] = kb.bump(
                            scalar.activation(qpre[:], cb_b[:], AF.Silu), "act", 1)

                    @sec("vector")
                    def _(vector):
                        kb.wait(vector, "vector", "act", mk["silu_done"])
                        kb.wait(vector, "vector", "pe", mk["sqv_free"])
                        mk["sq_done"] = kb.bump(
                            vector.tensor_tensor(sqv[:], qpre[:], qpre[:], ALU.mult), "dve", 1)

                    @sec("tensor")
                    def _(tensor):
                        kb.wait(tensor, "tensor", "dve", mk["sq_done"])
                        bs, bv = mk["bank_free"][7]
                        kb.wait(tensor, "tensor", bs, bv)
                        v = kb.bump(tensor.matmul(pb[7][0:2, 0:512], sbc["sel2_bf"][:],
                                                  sqv[:], start=True, stop=True), "pe", 1)
                        mk["rsq_mm"] = v
                        mk["sqv_free"] = v

                    @sec("scalar")
                    def _(scalar):
                        kb.wait(scalar, "scalar", "pe", mk["rsq_mm"])
                        v = kb.bump(scalar.activation(sd_row[:], pb[7][0:2, 0:512], AF.Sqrt,
                                                      bias=CV[0:2, 1:2]), "act", 1)
                        mk["sd_done"] = v
                        mk["bank_free"][7] = ("act", v)

                    @sec("vector")
                    def _(vector):
                        kb.wait(vector, "vector", "act", mk["sd_done"])
                        with nc.allow_low_precision("bf16 matmul operand (sanctioned)"):
                            ins = vector.reciprocal(rstd_bf[:], sd_row[:])
                        mk["rstd_done"] = kb.bump(ins, "dve", 1)

                    @sec("tensor")
                    def _(tensor):
                        kb.wait(tensor, "tensor", "dve", mk["rstd_done"])
                        bs, bv = mk["bank_free"][5]
                        kb.wait(tensor, "tensor", bs, bv)
                        mk["bcast_mm"] = kb.bump(
                            tensor.matmul(pb[5][:, 0:512], sbc["selT_bf"][:], rstd_bf[:],
                                          start=True, stop=True), "pe", 1)

                    @sec("vector")
                    def _(vector):
                        kb.wait(vector, "vector", "pe", mk["bcast_mm"])
                        kb.wait(vector, "vector", "tr", mk.get("tok_tr", 0))
                        v = kb.bump(vector.tensor_tensor(dst[:, oc * TT:(oc + 1) * TT],
                                                         qpre[:], pb[5][:, 0:512], ALU.mult),
                                    "dve", 1)
                        mk["bank_free"][5] = ("dve", v)
                        mk["qpre_free"] = v

                mk["wt_ready"] = kb.val("tr")
                for tile in range(NTILE):
                    b, tl = tile // 4, tile % 4
                    g0 = tile * TT
                    first = (tl == 0)

                    @sec("sync")
                    def _(sync, g0=g0):
                        kb.wait(sync, "sync", "tr", mk["wt_ready"])
                        kb.wait(sync, "sync", "pe", mk.get("proj_mm", 0))
                        mk["xnt_ld"] = kb.bump(sync.dma_start(
                            out=xnt[:].rearrange("p (c t) -> p c t", c=8),
                            in_=hb["XNBF"].rearrange("(c p) t -> p c t", p=128)[:, :, g0:g0 + TT]),
                            "ld", 16)

                    for pi, nm in [(0, "q"), (1, "k"), (2, "v"), (3, "g")]:
                        for oc in range(8):
                            B_chunk(pi if pi < 3 else 3, nm, oc, tile, first)

                    if debug_names & {"QD", "KD", "VD", "GD"}:
                        @sec("sync")
                        def _(sync, g0=g0):
                            kb.wait(sync, "sync", "dve")
                            kb.wait(sync, "sync", "act")
                            for dn, buf in [("QD", q_bf), ("KD", k_bf), ("VD", v_bf), ("GD", g_bf)]:
                                if dn in debug_names:
                                    kb.bump(sync.dma_start(
                                        out=hb[dn].rearrange("(c p) t -> p c t", p=128)[:, :, g0:g0 + TT],
                                        in_=buf[:].rearrange("p (c t) -> p c t", c=8)), "st", 16)

                    if first:
                        @sec("tensor")
                        def _(tensor):
                            kb.wait(tensor, "tensor", "act", mk["wbf_evac"])
                            kb.wait(tensor, "tensor", "ld")
                            for c in range(8):
                                tensor.matmul(pb[0][0:64, c * 64:(c + 1) * 64],
                                              sbc["ident_f32"][0:64, 0:64],
                                              W0_sb[0:64, c * 64:(c + 1) * 64],
                                              start=(c == 0), stop=(c == 7))
                            for c in range(8):
                                ins = tensor.matmul(pb[1][0:64, c * 64:(c + 1) * 64],
                                                    sbc["ident_f32"][64:128, 64:128],
                                                    W0_sb[64:128, c * 64:(c + 1) * 64],
                                                    start=(c == 0), stop=(c == 7))
                            mk["state_mm"] = kb.bump(ins, "pe", 1)

                        @sec("scalar")
                        def _(scalar):
                            kb.wait(scalar, "scalar", "pe", mk["state_mm"])
                            scalar.copy(W_bf[0:64, :], pb[0][0:64, 0:512])
                            mk["wbf_evac"] = kb.bump(
                                scalar.copy(W_bf[64:128, :], pb[1][0:64, 0:512]), "act", 1)

                    for mc in range(4):
                        par = mc % 2
                        oa, ob = (pb[2], pb[3]) if par == 0 else (pb[4], pb[7])

                        @sec("scalar")
                        def _(scalar, mc=mc, par=par):
                            kb.wait(scalar, "scalar", "dve")
                            kb.wait(scalar, "scalar", "act")
                            kb.wait(scalar, "scalar", "pe", mk["tok_free"][par])
                            for c in range(8):
                                scalar.dma_start_transpose(
                                    ktok[par][:, c * 128:(c + 1) * 128],
                                    k_bf[:, c * TT + mc * 128: c * TT + (mc + 1) * 128]).then_inc(kb.sems["tr"], 16)
                                scalar.dma_start_transpose(
                                    vtok[par][:, c * 128:(c + 1) * 128],
                                    v_bf[:, c * TT + mc * 128: c * TT + (mc + 1) * 128]).then_inc(kb.sems["tr"], 16)
                                scalar.dma_start_transpose(
                                    gtok[par][:, c * 128:(c + 1) * 128],
                                    g_bf[:, c * TT + mc * 128: c * TT + (mc + 1) * 128]).then_inc(kb.sems["tr"], 16)
                            kb.cnt["tr"] += 16 * 24
                            mk["tok_tr"] = kb.val("tr")

                        for c in range(8):
                            @sec("tensor")
                            def _(tensor, c=c, mc=mc, par=par, oa=oa, ob=ob):
                                kb.wait(tensor, "tensor", "tr", mk["tok_tr"])
                                kb.wait(tensor, "tensor", "act", mk["wbf_evac"])
                                kb.wait(tensor, "tensor", "dve", mk.get("mask_done", 0))
                                kb.wait(tensor, "tensor", "dve", mk["o_epi"][par])
                                col = c * TT + mc * 128
                                tensor.matmul(pb[5][:, 0:128], k_bf[0:64, col:col + 128],
                                              q_bf[0:64, col:col + 128], start=True, stop=True)
                                mk["st_mm"] = kb.bump(
                                    tensor.matmul(pb[6][:, 0:128], k_bf[64:128, col:col + 128],
                                                  q_bf[64:128, col:col + 128], start=True, stop=True),
                                    "pe", 1)

                            @sec("vector")
                            def _(vector, c=c):
                                kb.wait(vector, "vector", "pe", mk["st_mm"])
                                vector.tensor_tensor(ptT[0][:], pb[5][:, 0:128],
                                                     sbc["maskTT_bf"][:], ALU.mult)
                                mk["mask_done"] = kb.bump(
                                    vector.tensor_tensor(ptT[1][:], pb[6][:, 0:128],
                                                         sbc["maskTT_bf"][:], ALU.mult), "dve", 1)

                            @sec("tensor")
                            def _(tensor, c=c, mc=mc, par=par, oa=oa, ob=ob):
                                kb.wait(tensor, "tensor", "dve", mk["mask_done"])
                                col = c * TT + mc * 128
                                ocol = c * 64
                                tensor.matmul(oa[:, ocol:ocol + 64], q_bf[0:64, col:col + 128],
                                              W_bf[0:64, ocol:ocol + 64],
                                              start=(c == 0), stop=False)
                                tensor.matmul(ob[:, ocol:ocol + 64], q_bf[64:128, col:col + 128],
                                              W_bf[64:128, ocol:ocol + 64],
                                              start=(c == 0), stop=False)
                                tensor.matmul(oa[:, ocol:ocol + 64], ptT[0][:],
                                              vtok[par][:, c * 128: c * 128 + 64],
                                              start=False, stop=(c == 7))
                                tensor.matmul(ob[:, ocol:ocol + 64], ptT[1][:],
                                              vtok[par][:, c * 128 + 64: c * 128 + 128],
                                              start=False, stop=(c == 7))
                                tensor.matmul(pb[0][0:64, ocol:ocol + 64],
                                              ktok[par][:, c * 128: c * 128 + 64],
                                              vtok[par][:, c * 128: c * 128 + 64],
                                              start=False, stop=False)
                                ins = tensor.matmul(pb[1][0:64, ocol:ocol + 64],
                                                    ktok[par][:, c * 128 + 64: c * 128 + 128],
                                                    vtok[par][:, c * 128 + 64: c * 128 + 128],
                                                    start=False, stop=False)
                                v = kb.bump(ins, "pe", 1)
                                mk["pair_mm"] = v
                                if c == 7:
                                    mk["scan_mm"] = v
                                    mk["tok_free"][par] = v

                        @sec("scalar")
                        def _(scalar, mc=mc):
                            kb.wait(scalar, "scalar", "pe", mk["scan_mm"])
                            scalar.copy(W_bf[0:64, :], pb[0][0:64, 0:512])
                            mk["wbf_evac"] = kb.bump(
                                scalar.copy(W_bf[64:128, :], pb[1][0:64, 0:512]), "act", 1)

                        @sec("vector")
                        def _(vector, mc=mc, par=par, oa=oa, ob=ob):
                            kb.wait(vector, "vector", "pe", mk["scan_mm"])
                            kb.wait(vector, "vector", "dve")
                            for h2 in range(8):
                                vector.bn_stats(mvstat[:, h2 * 6:(h2 + 1) * 6],
                                                oa[:, h2 * 64:(h2 + 1) * 64])
                            for h2 in range(8):
                                ins = vector.bn_stats(mvstat[:, 48 + h2 * 6: 48 + (h2 + 1) * 6],
                                                      ob[:, h2 * 64:(h2 + 1) * 64])
                            kb.fence(vector, "vector", ins, "dve")
                            for h in range(16):
                                base = (h % 2) * 48 + (h // 2) * 6
                                ins = vector.bn_aggr(mv[:, h * 2:(h + 1) * 2],
                                                     mvstat[:, base:base + 6])
                            mk["aggr_done"] = kb.bump(ins, "dve", 1)

                        @sec("scalar")
                        def _(scalar, mc=mc):
                            kb.wait(scalar, "scalar", "dve", mk["aggr_done"])
                            mk["osd"] = kb.bump(
                                scalar.activation(sdv[:], mv[:, 1:32:2], AF.Sqrt,
                                                  bias=CV[:, 0:1]), "act", 1)

                        @sec("vector")
                        def _(vector, mc=mc, par=par, oa=oa, ob=ob):
                            kb.wait(vector, "vector", "act", mk["osd"])
                            kb.fence(vector, "vector", vector.reciprocal(rinv[:], sdv[:]), "dve")
                            for h in range(16):
                                bank = oa if h % 2 == 0 else ob
                                hcol = (h // 2) * 64
                                ins = vector.tensor_scalar(
                                    t_f[:, h * 64:(h + 1) * 64], bank[:, hcol:hcol + 64],
                                    mv[:, 2 * h:2 * h + 1], rinv[:, h:h + 1],
                                    ALU.subtract, ALU.mult)
                            v = kb.bump(ins, "dve", 1)
                            mk["o_epi"][par] = v
                            kb.wait(vector, "vector", "dve", v)
                            for h in range(16):
                                ins = vector.tensor_tensor(
                                    t2_f[:, h * 64:(h + 1) * 64],
                                    t_f[:, h * 64:(h + 1) * 64], gam_bc[:], ALU.mult)
                            kb.fence(vector, "vector", ins, "dve")
                            for h in range(16):
                                ins = vector.tensor_tensor(
                                    o_all[:, mc * 1024 + h * 64: mc * 1024 + (h + 1) * 64],
                                    t2_f[:, h * 64:(h + 1) * 64], bet_bc[:], ALU.add)
                            kb.fence(vector, "vector", ins, "dve")
                            kb.wait(vector, "vector", "tr", mk["tok_tr"])
                            v = kb.bump(vector.tensor_tensor(
                                og[:, mc * 1024:(mc + 1) * 1024],
                                o_all[:, mc * 1024:(mc + 1) * 1024], gtok[par][:], ALU.mult),
                                "dve", 1)
                            mk["og_done"][mc] = v

                        if "OD" in debug_names:
                            @sec("sync")
                            def _(sync, mc=mc, g0=g0):
                                kb.wait(sync, "sync", "dve", mk["og_done"][mc])
                                kb.bump(sync.dma_start(
                                    out=hb["OD"][g0 + mc * 128: g0 + (mc + 1) * 128, :],
                                    in_=o_all[:, mc * 1024:(mc + 1) * 1024]), "st", 16)

                        @sec("scalar")
                        def _(scalar, mc=mc):
                            kb.wait(scalar, "scalar", "dve", mk["og_done"][mc])
                            kb.wait(scalar, "scalar", "pe", mk.get("s_mm", 0))
                            for c in range(8):
                                scalar.dma_start_transpose(
                                    og_fm[:, c * TT + mc * 128: c * TT + (mc + 1) * 128],
                                    og[:, mc * 1024 + c * 128: mc * 1024 + (c + 1) * 128]).then_inc(kb.sems["tr"], 16)
                            kb.cnt["tr"] += 16 * 8
                            mk["ogfm_tr"] = kb.val("tr")

                    mk["bank_free"][2] = ("dve", mk["o_epi"][0])
                    mk["bank_free"][3] = ("dve", mk["o_epi"][0])
                    mk["bank_free"][4] = ("dve", mk["o_epi"][1])
                    mk["bank_free"][7] = ("dve", mk["o_epi"][1])

                    for oc in range(8):
                        bslot = 2 + (rot_counter[0] % 3)
                        rot_counter[0] += 1
                        bank = pb[bslot]

                        @sec("tensor")
                        def _(tensor, oc=oc, bslot=bslot, bank=bank):
                            kb.wait(tensor, "tensor", "tr", mk["ogfm_tr"])
                            kb.wait(tensor, "tensor", "dve", mk["o_epi"][0])
                            kb.wait(tensor, "tensor", "dve", mk["o_epi"][1])
                            bs, bv = mk["bank_free"][bslot] if bslot in mk["bank_free"] else ("pe", 0)
                            kb.wait(tensor, "tensor", bs, bv)
                            for kc in range(8):
                                ins = tensor.matmul(bank[:, 0:512],
                                                    WT["o"][:, kc * 1024 + oc * 128: kc * 1024 + (oc + 1) * 128],
                                                    og_fm[:, kc * TT:(kc + 1) * TT],
                                                    start=(kc == 0), stop=(kc == 7))
                            mk["s_mm"] = kb.bump(ins, "pe", 1)

                        @sec("scalar")
                        def _(scalar, oc=oc, bslot=bslot, bank=bank):
                            kb.wait(scalar, "scalar", "pe", mk["s_mm"])
                            kb.wait(scalar, "scalar", "st", mk["sst_free"][oc % 2])
                            v = kb.bump(scalar.copy(s_stage[oc % 2][:], bank[:, 0:512]), "act", 1)
                            mk["s_evac"] = v
                            mk["bank_free"][bslot] = ("act", v)

                        @sec("sync")
                        def _(sync, oc=oc, g0=g0):
                            kb.wait(sync, "sync", "act", mk["s_evac"])
                            mk["sst_free"][oc % 2] = kb.bump(sync.dma_start(
                                out=hb["S32"][oc * 128:(oc + 1) * 128, g0:g0 + TT],
                                in_=s_stage[oc % 2][:]), "st", 16)

        if "p2" in phases:
            with ExitStack() as wcx:
                wstg_f = [wcx.enter_context(nc.sbuf_tensor(f"wc_f{i}", [128, 4096], F32))
                          for i in range(2)]
                wstg_b = [wcx.enter_context(nc.sbuf_tensor(f"wc_b{i}", [128, 4096], BF16))
                          for i in range(2)]
                fqkv2 = wt["fWqkv"].rearrange("l r c -> (l r) c")
                fwo2 = wt["fWo"].rearrange("l r c -> (l r) c")
                fw12 = wt["fW1"].rearrange("l r c -> (l r) c")
                fw22 = wt["fW2"].rearrange("l r c -> (l r) c")
                jobs = []
                for L in range(depth):
                    for cch in range(NCH):
                        jobs.append((fqkv2, hb["WQKVB"], L * D + cch * 128, 128, 0, 3072))
                        jobs.append((fwo2, hb["WOB"], L * D + cch * 128, 128, 0, 1024))
                        jobs.append((fw12, hb["W1B"], L * D + cch * 128, 128, 0, 4096))
                        jobs.append((fw12, hb["W1B"], L * D + cch * 128, 128, 4096, 1364))
                    for off, ln in W2CH:
                        jobs.append((fw22, hb["W2B"], L * HALF1 + off, ln, 0, 1024))
                ldm = {}
                stm = {}
                for n in range(len(jobs) + 1):
                    @sec("gpsimd")
                    def _(gpsimd, n=n):
                        if n == 0:
                            for s in ["pe", "dve", "act", "st", "ld"]:
                                kb.wait(gpsimd, "gpsimd", s)
                        if n < len(jobs):
                            srcT, dstT, r0, rl, c0, w = jobs[n]
                            if n >= 2:
                                kb.wait(gpsimd, "gpsimd", "wt", stm[n - 2])
                            ldm[n] = kb.bump(gpsimd.dma_start(
                                out=wstg_f[n % 2][0:rl, 0:w],
                                in_=srcT[r0:r0 + rl, c0:c0 + w]), "wt", 16)
                        if n >= 1:
                            m = n - 1
                            srcT, dstT, r0, rl, c0, w = jobs[m]
                            kb.wait(gpsimd, "gpsimd", "wt", ldm[m])
                            kb.fence(gpsimd, "gpsimd",
                                     gpsimd.tensor_copy(wstg_b[m % 2][0:rl, 0:w],
                                                        wstg_f[m % 2][0:rl, 0:w]), "pool")
                            stm[m] = kb.bump(gpsimd.dma_start(
                                out=dstT[r0:r0 + rl, c0:c0 + w],
                                in_=wstg_b[m % 2][0:rl, 0:w]), "wt", 16)
            precast_done = kb.val("wt")

        if "p2" in phases:
            n_sub = 2 * depth if p2_sub is None else p2_sub
            masters = [("XN32", "XNBF"), ("XA32", "XABF"), ("XB32", "XBBF")]

            with ExitStack() as octx:
                sbo = lambda name, shape, dt: octx.enter_context(nc.sbuf_tensor(name, shape, dt))
                xs_bf = sbo("p2_xs", [128, 8 * 513], BF16)
                res_f = [sbo(f"p2_res{i}", [128, 512], F32) for i in range(2)]
                xnew_f = [sbo(f"p2_xnf{i}", [128, 512], F32) for i in range(2)]
                xnew_b = [sbo(f"p2_xnb{i}", [128, 512], BF16) for i in range(2)]
                scr_f = sbo("p2_scrf", [128, 2 * 512], F32)
                scr_b = sbo("p2_scrb", [128, 2 * 512], BF16)
                rowa2 = sbo("p2_rowa2", [2, 512], F32)
                rowb2 = sbo("p2_rowb2", [2, 512], F32)
                rowbf2 = sbo("p2_rowbf2", [2, 512], BF16)
                rowa = sbo("p2_rowa", [1, 1024], F32)
                rowb = sbo("p2_rowb", [1, 1024], F32)

                pm = {"xs_ld": 0, "res_ld": [0, 0], "xnew_free": [0, 0],
                      "bank": {i: ("pe", 0) for i in range(8)},
                      "sq_free": 0, "scr_free": 0}

                def bank_wait(tensor, i):
                    s, v = pm["bank"][i]
                    kb.wait(tensor, "tensor", s, v)

                def emit_attn_sub(L, s_idx):
                    in32, inbf = masters[0] if s_idx == 0 else masters[1 + ((s_idx - 1) % 2)]
                    o32, obf = masters[1 + (s_idx % 2)]
                    with ExitStack() as actx:
                        sba = lambda name, shape, dt: actx.enter_context(
                            nc.sbuf_tensor(f"L{L}a_{name}", shape, dt))
                        Wqkv_sb = sba("wqkv", [128, 8 * 3072], BF16)
                        Wo_sb = sba("wo", [128, 8 * 1024], BF16)
                        q_b = sba("q", [128, 8 * 512], BF16)
                        k_b = [sba(f"k{i}", [128, 8 * 512], BF16) for i in range(2)]
                        v_b = sba("v", [128, 8 * 512], BF16)
                        vt_tmp = sba("vtt", [128, 1024], BF16)
                        v_tok = [sba(f"vtok{i}", [128, 4 * 1040], BF16) for i in range(2)]
                        P_sb = [sba(f"ptp{i}", [128, 384], BF16) for i in range(2)]
                        og_b = sba("og", [128, 8 * 512], BF16)
                        og_t = sba("ogt", [64, 512], BF16)
                        o_f = [sba(f"of{i}", [128, 512], F32) for i in range(2)]
                        l_f = sba("lf", [1, 1024], F32)
                        linv = sba("linv", [1, 1024], BF16)

                        @sec("sync")
                        def _(sync):
                            kb.wait(sync, "sync", "wt", precast_done)
                            kb.wait(sync, "sync", "pe")
                            kb.wait(sync, "sync", "dve")
                            kb.wait(sync, "sync", "act")
                            kb.wait(sync, "sync", "tr")
                            kb.wait(sync, "sync", "st")
                            for cch in range(NCH):
                                kb.bump(sync.dma_start(
                                    out=Wqkv_sb[:, cch * 3072:(cch + 1) * 3072],
                                    in_=hb["WQKVB"][L * D + cch * 128: L * D + (cch + 1) * 128, :]),
                                    "wt", 16)
                            for cch in range(NCH):
                                kb.bump(sync.dma_start(
                                    out=Wo_sb[:, cch * 1024:(cch + 1) * 1024],
                                    in_=hb["WOB"][L * D + cch * 128: L * D + (cch + 1) * 128, :]),
                                    "wt", 16)
                            kb.bump(sync.dma_start(out=qs_col[0:64, 0:1], in_=wt["fqs"][L, :, None]), "wt", 16)
                            kb.bump(sync.dma_start(out=qs_col[64:128, 0:1], in_=wt["fqs"][L, :, None]), "wt", 16)
                            kb.bump(sync.dma_start(out=ks_col[0:64, 0:1], in_=wt["fks"][L, :, None]), "wt", 16)
                            kb.bump(sync.dma_start(out=ks_col[64:128, 0:1], in_=wt["fks"][L, :, None]), "wt", 16)
                            for cch in range(NCH):
                                kb.bump(sync.dma_start(out=lng_cols[:, cch:cch + 1],
                                                       in_=wt["flng"][L, cch * 128:(cch + 1) * 128, None]), "wt", 16)
                                kb.bump(sync.dma_start(out=lnb_cols[:, cch:cch + 1],
                                                       in_=wt["flnb"][L, cch * 128:(cch + 1) * 128, None]), "wt", 16)
                                kb.bump(sync.dma_start(out=b2_cols[:, cch:cch + 1],
                                                       in_=wt["fb2"][L, cch * 128:(cch + 1) * 128, None]), "wt", 16)
                            for ci, (off, ln) in enumerate(W2CH):
                                kb.bump(sync.dma_start(out=b1_cols[0:ln, ci:ci + 1],
                                                       in_=wt["fb1"][L, off:off + ln, None]), "wt", 16)
                                kb.bump(sync.dma_start(out=b1_cols[0:ln, 22 + ci:23 + ci],
                                                       in_=wt["fb1"][L, HALF1 + off:HALF1 + off + ln, None]), "wt", 16)
                            pm["wt_ld"] = kb.val("wt")
                        @sec("vector")
                        def _(vector):
                            for par in range(2):
                                ins = vector.memset(
                                    v_tok[par][:].rearrange("p (b h e) -> p b h e", h=16, e=65)[:, :, :, 64:65],
                                    1.0)
                            pm["vones"] = kb.bump(ins, "dve", 1)

                        for tile in range(NTILE):
                            b, tl = tile // 4, tile % 4
                            g0 = tile * TT
                            par = tile % 2
                            tab0 = tl * TT

                            @sec("sync")
                            def _(sync, g0=g0, tl=tl):
                                kb.wait(sync, "sync", "pe", pm.get("qkv_mm", 0))
                                kb.wait(sync, "sync", "dve", pm.get("xs_dve", 0))
                                if tl == 0:
                                    pm["xs_ld"] = kb.bump(sync.dma_start(
                                        out=xs_bf[:].rearrange("p (c t) -> p c t", c=8)[:, :, 1:513],
                                        in_=hb[inbf].rearrange("(c p) t -> p c t", p=128)[:, :, g0:g0 + TT]),
                                        "ld", 16)
                                else:
                                    pm["xs_ld"] = kb.bump(sync.dma_start(
                                        out=xs_bf[:].rearrange("p (c t) -> p c t", c=8),
                                        in_=hb[inbf].rearrange("(c p) t -> p c t", p=128)[:, :, g0 - 1:g0 + TT]),
                                        "ld", 16)

                            if tl == 0:
                                @sec("vector")
                                def _(vector):
                                    kb.wait(vector, "vector", "ld", pm["xs_ld"])
                                    pm["xs_dve"] = kb.bump(vector.memset(
                                        xs_bf[:].rearrange("p (c t) -> p c t", c=8)[:, :, 0:1], 0.0),
                                        "dve", 1)

                            def xs_slice(kc):
                                if kc < 4:
                                    return xs_bf[:, kc * 513 + 1: kc * 513 + 513]
                                return xs_bf[:, kc * 513: kc * 513 + 512]

                            for j in range(24):
                                nm = "q" if j < 8 else ("k" if j < 16 else "v")
                                oc = j % 8
                                bslot = j % 3
                                bank = pb[bslot]

                                @sec("tensor")
                                def _(tensor, j=j, bslot=bslot, bank=bank, tl=tl):
                                    kb.wait(tensor, "tensor", "ld", pm["xs_ld"])
                                    if tl == 0:
                                        kb.wait(tensor, "tensor", "dve", pm["xs_dve"])
                                    kb.wait(tensor, "tensor", "wt", pm["wt_ld"])
                                    bank_wait(tensor, bslot)
                                    for kc in range(8):
                                        ins = tensor.matmul(
                                            bank[:, 0:512],
                                            Wqkv_sb[:, kc * 3072 + j * 128: kc * 3072 + (j + 1) * 128],
                                            xs_slice(kc), start=(kc == 0), stop=(kc == 7))
                                    pm["qkv_mm"] = kb.bump(ins, "pe", 1)

                                if nm == "v":
                                    @sec("scalar")
                                    def _(scalar, oc=oc, bslot=bslot, bank=bank):
                                        kb.wait(scalar, "scalar", "pe", pm["qkv_mm"])
                                        kb.wait(scalar, "scalar", "tr", pm.get("vt_tr", 0))
                                        v = kb.bump(scalar.copy(v_b[:, oc * 512:(oc + 1) * 512],
                                                                bank[:, 0:512]), "act", 1)
                                        pm["bank"][bslot] = ("act", v)
                                    continue

                                dst = q_b if nm == "q" else k_b[par]
                                tabs = (sbc["rotA"], sbc["rotB"]) if nm == "q" else (sbc["rotC"], sbc["rotD"])
                                scol = qs_col if nm == "q" else ks_col

                                @sec("scalar")
                                def _(scalar, oc=oc, bslot=bslot, bank=bank):
                                    kb.wait(scalar, "scalar", "pe", pm["qkv_mm"])
                                    kb.wait(scalar, "scalar", "dve", pm.get("rote_free", 0))
                                    v = kb.bump(scalar.copy(scr_b[:, 0:512], bank[:, 0:512]), "act", 1)
                                    pm["qevac"] = v
                                    pm["bank"][bslot] = ("act", v)

                                @sec("vector")
                                def _(vector):
                                    kb.wait(vector, "vector", "act", pm["qevac"])
                                    kb.wait(vector, "vector", "pe", pm.get("rsq_mm", 0))
                                    pm["sq_done"] = kb.bump(
                                        vector.tensor_tensor(scr_b[:, 512:1024], scr_b[:, 0:512],
                                                             scr_b[:, 0:512], ALU.mult), "dve", 1)

                                @sec("tensor")
                                def _(tensor):
                                    kb.wait(tensor, "tensor", "dve", pm["sq_done"])
                                    bank_wait(tensor, 3)
                                    pm["rsq_mm"] = kb.bump(
                                        tensor.matmul(pb[3][0:2, 0:512], sbc["sel2_bf"][:],
                                                      scr_b[:, 512:1024], start=True, stop=True), "pe", 1)

                                @sec("scalar")
                                def _(scalar):
                                    kb.wait(scalar, "scalar", "pe", pm["rsq_mm"])
                                    kb.wait(scalar, "scalar", "dve", pm.get("nrm_r", 0))
                                    v = kb.bump(scalar.activation(rowa2[:], pb[3][0:2, 0:512],
                                                                  AF.Sqrt), "act", 1)
                                    pm["nrm_sd"] = v
                                    pm["bank"][3] = ("act", v)

                                @sec("vector")
                                def _(vector):
                                    kb.wait(vector, "vector", "act", pm["nrm_sd"])
                                    kb.fence(vector, "vector",
                                             vector.tensor_scalar(rowb2[:], rowa2[:], 1e-12, None,
                                                                  ALU.max), "dve")
                                    with nc.allow_low_precision("bf16 matmul operand"):
                                        ins = vector.reciprocal(rowbf2[:], rowb2[:])
                                    pm["nrm_r"] = kb.bump(ins, "dve", 1)

                                @sec("tensor")
                                def _(tensor):
                                    kb.wait(tensor, "tensor", "dve", pm["nrm_r"])
                                    bank_wait(tensor, 4)
                                    pm["bcast_mm"] = kb.bump(
                                        tensor.matmul(pb[4][:, 0:512], sbc["selT_bf"][:],
                                                      rowbf2[:], start=True, stop=True), "pe", 1)
                                    pm["bank"][4] = ("pe", pm["bcast_mm"])

                                @sec("vector")
                                def _(vector, scol=scol):
                                    kb.wait(vector, "vector", "pe", pm["bcast_mm"])
                                    pm["qn_done"] = kb.bump(vector.scalar_tensor_tensor(
                                        scr_b[:, 512:1024], scr_b[:, 0:512], scol[:],
                                        pb[4][:, 0:512], ALU.mult, ALU.mult), "dve", 1)
                                pm["bank"][4] = ("dve", pm["qn_done"])

                                @sec("tensor")
                                def _(tensor):
                                    kb.wait(tensor, "tensor", "dve", pm["qn_done"])
                                    bank_wait(tensor, 5)
                                    pm["rot_mm"] = kb.bump(
                                        tensor.matmul(pb[5][:, 0:512], sbc["R2T_bf"][:],
                                                      scr_b[:, 512:1024], start=True, stop=True), "pe", 1)

                                @sec("vector")
                                def _(vector, oc=oc, tabs=tabs, dst=dst, tab0=tab0):
                                    kb.wait(vector, "vector", "pe", pm["rot_mm"])
                                    vector.tensor_tensor(scr_b[:, 0:512], scr_b[:, 512:1024],
                                                         tabs[0][:, tab0:tab0 + 512], ALU.mult)
                                    kb.fence(vector, "vector", vector.tensor_tensor(
                                        scr_f[:, 0:512], pb[5][:, 0:512],
                                        tabs[1][:, tab0:tab0 + 512], ALU.mult), "dve")
                                    v = kb.bump(vector.tensor_tensor(
                                        dst[:, oc * 512:(oc + 1) * 512], scr_b[:, 0:512],
                                        scr_f[:, 0:512], ALU.add), "dve", 1)
                                    pm["rote_free"] = v
                                    pm["bank"][5] = ("dve", v)

                            for blk in range(4):
                                @sec("scalar")
                                def _(scalar, blk=blk, par=par):
                                    kb.wait(scalar, "scalar", "act")
                                    kb.wait(scalar, "scalar", "dve", pm.get("vtok_cp", 0))
                                    kb.wait(scalar, "scalar", "dve", pm["vones"])
                                    kb.wait(scalar, "scalar", "pe", pm.get("pv_mm", 0))
                                    for cpair in range(8):
                                        scalar.dma_start_transpose(
                                            vt_tmp[:, cpair * 128:(cpair + 1) * 128],
                                            v_b[:, cpair * 512 + blk * 128: cpair * 512 + (blk + 1) * 128]
                                        ).then_inc(kb.sems["tr"], 16)
                                    kb.cnt["tr"] += 16 * 8
                                    pm["vt_tr"] = kb.val("tr")

                                @sec("vector")
                                def _(vector, blk=blk, par=par):
                                    kb.wait(vector, "vector", "tr", pm["vt_tr"])
                                    pm["vtok_cp"] = kb.bump(vector.tensor_copy(
                                        v_tok[par][:].rearrange("p (b h e) -> p b h e", h=16, e=65)[:, blk, :, 0:64],
                                        vt_tmp[:].rearrange("p (h e) -> p h e", e=64)), "dve", 1)

                            cmin = 0 if tl == 0 else -2
                            for hp in range(8):
                                h0, h1 = 2 * hp, 2 * hp + 1
                                for c in range(cmin, 4):
                                    j0 = max(0, c)
                                    j1 = min(3, c + 2)
                                    ncols = (j1 - j0 + 1) * 128
                                    qoff = j0 * 128
                                    moff = (j0 - c) * 128
                                    if c >= 0:
                                        kb_buf, kcol = k_b[par], c * 128
                                        vt_buf, vblk = v_tok[par], c
                                    else:
                                        kb_buf, kcol = k_b[1 - par], (4 + c) * 128
                                        vt_buf, vblk = v_tok[1 - par], 4 + c

                                    @sec("tensor")
                                    def _(tensor, hp=hp, c=c, j0=j0, ncols=ncols, qoff=qoff,
                                          moff=moff, kb_buf=kb_buf, kcol=kcol):
                                        kb.wait(tensor, "tensor", "dve", pm.get("vtok_cp", 0))
                                        kb.wait(tensor, "tensor", "act", pm.get("exp_done", 0))
                                        if c == cmin:
                                            bank_wait(tensor, 5)
                                            kb.wait(tensor, "tensor", "dve", pm.get("ogTT", 0))
                                        tensor.matmul(pb[5][:, 0:ncols],
                                                      kb_buf[0:64, hp * 512 + kcol: hp * 512 + kcol + 128],
                                                      q_b[0:64, hp * 512 + qoff: hp * 512 + qoff + ncols],
                                                      start=True, stop=False)
                                        tensor.matmul(pb[6][:, 0:ncols],
                                                      kb_buf[64:128, hp * 512 + kcol: hp * 512 + kcol + 128],
                                                      q_b[64:128, hp * 512 + qoff: hp * 512 + qoff + ncols],
                                                      start=True, stop=False)
                                        tensor.matmul(pb[5][:, 0:ncols], sbc["negI_bf"][:],
                                                      sbc["maskU_bf"][:, moff:moff + ncols],
                                                      start=False, stop=True)
                                        pm["st_mm"] = kb.bump(
                                            tensor.matmul(pb[6][:, 0:ncols], sbc["negI_bf"][:],
                                                          sbc["maskU_bf"][:, moff:moff + ncols],
                                                          start=False, stop=True), "pe", 1)

                                    @sec("scalar")
                                    def _(scalar, ncols=ncols):
                                        kb.wait(scalar, "scalar", "pe", pm["st_mm"])
                                        kb.wait(scalar, "scalar", "pe", pm.get("pv_mm", 0))
                                        scalar.activation(P_sb[0][:, 0:ncols], pb[5][:, 0:ncols],
                                                          AF.Exp, bias=CV[:, 2:3], scale=8.0)
                                        pm["exp_done"] = kb.bump(
                                            scalar.activation(P_sb[1][:, 0:ncols], pb[6][:, 0:ncols],
                                                              AF.Exp, bias=CV[:, 2:3], scale=8.0), "act", 1)

                                    @sec("tensor")
                                    def _(tensor, hp=hp, c=c, j0=j0, j1=j1, ncols=ncols,
                                          qoff=qoff, vt_buf=vt_buf, vblk=vblk, h0=h0, h1=h1):
                                        kb.wait(tensor, "tensor", "act", pm["exp_done"])
                                        if c == cmin:
                                            kb.wait(tensor, "tensor", "act", pm.get("oevac", 0))
                                            kb.wait(tensor, "tensor", "act", pm.get("levac", 0))
                                            bank_wait(tensor, 3)
                                            bank_wait(tensor, 4)
                                        tensor.matmul(pb[3][0:65, qoff:qoff + ncols],
                                                      vt_buf[:, vblk * 1040 + h0 * 65: vblk * 1040 + h0 * 65 + 65],
                                                      P_sb[0][:, 0:ncols],
                                                      start=(c == cmin), stop=(c == 3))
                                        pm["pv_mm"] = kb.bump(tensor.matmul(
                                            pb[4][0:65, qoff:qoff + ncols],
                                            vt_buf[:, vblk * 1040 + h1 * 65: vblk * 1040 + h1 * 65 + 65],
                                            P_sb[1][:, 0:ncols],
                                            start=(c == cmin), stop=(c == 3)), "pe", 1)

                                @sec("scalar")
                                def _(scalar, hp=hp):
                                    kb.wait(scalar, "scalar", "pe", pm["pv_mm"])
                                    kb.wait(scalar, "scalar", "dve", pm.get("ogTT", 0))
                                    scalar.copy(l_f[0:1, 0:512], pb[3][64:65, 0:512])
                                    pm["levac"] = kb.bump(scalar.copy(l_f[0:1, 512:1024],
                                                                      pb[4][64:65, 0:512]), "act", 1)
                                    scalar.copy(o_f[0][0:64, :], pb[3][0:64, 0:512])
                                    pm["oevac"] = kb.bump(scalar.copy(o_f[1][0:64, :], pb[4][0:64, 0:512]),
                                                          "act", 1)

                                @sec("vector")
                                def _(vector, hp=hp):
                                    kb.wait(vector, "vector", "act", pm["levac"])
                                    kb.wait(vector, "vector", "pe", pm.get("lb_mm", 0))
                                    with nc.allow_low_precision("bf16 matmul operand"):
                                        ins = vector.reciprocal(linv[:], l_f[:])
                                    pm["linv"] = kb.bump(ins, "dve", 1)

                                @sec("tensor")
                                def _(tensor, hp=hp):
                                    kb.wait(tensor, "tensor", "dve", pm["linv"])
                                    bank_wait(tensor, 0)
                                    bank_wait(tensor, 1)
                                    tensor.matmul(pb[0][0:64, 0:512], sbc["ones_row_bf"][0:1, 0:64],
                                                  linv[0:1, 0:512], start=True, stop=True)
                                    pm["lb_mm"] = kb.bump(
                                        tensor.matmul(pb[1][0:64, 0:512], sbc["ones_row_bf"][0:1, 0:64],
                                                      linv[0:1, 512:1024], start=True, stop=True),
                                        "pe", 1)

                                @sec("vector")
                                def _(vector, hp=hp):
                                    kb.wait(vector, "vector", "pe", pm["lb_mm"])
                                    kb.wait(vector, "vector", "act", pm["oevac"])
                                    kb.wait(vector, "vector", "act", pm.get("og_cp", 0))
                                    vector.tensor_tensor(og_b[0:64, hp * 512:(hp + 1) * 512],
                                                         o_f[0][0:64, :], pb[0][0:64, 0:512], ALU.mult)
                                    v = kb.bump(vector.tensor_tensor(
                                        og_t[0:64, :],
                                        o_f[1][0:64, :], pb[1][0:64, 0:512], ALU.mult), "dve", 1)
                                    pm["ogTT"] = v
                                    pm["bank"][0] = ("dve", v)
                                    pm["bank"][1] = ("dve", v)

                                @sec("scalar")
                                def _(scalar, hp=hp):
                                    kb.wait(scalar, "scalar", "dve", pm["ogTT"])
                                    v = kb.bump(scalar.copy(og_b[64:128, hp * 512:(hp + 1) * 512],
                                                            og_t[0:64, :]), "act", 1)
                                    pm["og_cp"] = v
                                    if hp == 7:
                                        pm["og_all_act"] = v

                            if debug_names & {"Q2D", "K2D", "V2D", "OGD"}:
                                @sec("sync")
                                def _(sync, g0=g0, par=par):
                                    kb.wait(sync, "sync", "dve")
                                    kb.wait(sync, "sync", "act")
                                    for dn, buf in [("Q2D", q_b), ("K2D", k_b[par]),
                                                    ("V2D", v_b), ("OGD", og_b)]:
                                        if dn in debug_names:
                                            kb.bump(sync.dma_start(
                                                out=hb[dn].rearrange("(c p) t -> p c t", p=128)[:, :, g0:g0 + TT],
                                                in_=buf[:].rearrange("p (c t) -> p c t", c=8)), "st", 16)

                            for oc in range(8):
                                bslot = 1 + (oc % 2)
                                bank = pb[bslot]

                                @sec("sync")
                                def _(sync, oc=oc, g0=g0):
                                    kb.wait(sync, "sync", "dve", pm["xnew_free"][oc % 2])
                                    pm["res_ld"][oc % 2] = kb.bump(sync.dma_start(
                                        out=res_f[oc % 2][:],
                                        in_=hb[in32][oc * 128:(oc + 1) * 128, g0:g0 + TT]), "ld", 16)

                                @sec("tensor")
                                def _(tensor, oc=oc, bslot=bslot, bank=bank):
                                    kb.wait(tensor, "tensor", "dve", pm["ogTT"])
                                    kb.wait(tensor, "tensor", "act", pm["og_all_act"])
                                    bank_wait(tensor, bslot)
                                    for kc in range(8):
                                        ins = tensor.matmul(
                                            bank[:, 0:512],
                                            Wo_sb[:, kc * 1024 + oc * 128: kc * 1024 + (oc + 1) * 128],
                                            og_b[:, kc * 512:(kc + 1) * 512],
                                            start=(kc == 0), stop=(kc == 7))
                                    pm["wo_mm"] = kb.bump(ins, "pe", 1)

                                @sec("vector")
                                def _(vector, oc=oc, bslot=bslot, bank=bank):
                                    kb.wait(vector, "vector", "pe", pm["wo_mm"])
                                    kb.wait(vector, "vector", "ld", pm["res_ld"][oc % 2])
                                    kb.wait(vector, "vector", "st", pm.get("xst_%d" % (oc % 2), 0))
                                    v = kb.bump(vector.tensor_tensor(
                                        xnew_f[oc % 2][:], bank[:, 0:512], res_f[oc % 2][:], ALU.add),
                                        "dve", 1)
                                    pm["bank"][bslot] = ("dve", v)
                                    kb.wait(vector, "vector", "dve", v)
                                    pm["xnew_dve"] = kb.bump(vector.tensor_copy(
                                        xnew_b[oc % 2][:], xnew_f[oc % 2][:]), "dve", 1)
                                    pm["xnew_free"][oc % 2] = pm["xnew_dve"]

                                @sec("sync")
                                def _(sync, oc=oc, g0=g0):
                                    kb.wait(sync, "sync", "dve", pm["xnew_dve"])
                                    kb.bump(sync.dma_start(
                                        out=hb[o32][oc * 128:(oc + 1) * 128, g0:g0 + TT],
                                        in_=xnew_f[oc % 2][:]), "st", 16)
                                    pm["xst_%d" % (oc % 2)] = kb.bump(sync.dma_start(
                                        out=hb[obf][oc * 128:(oc + 1) * 128, g0:g0 + TT],
                                        in_=xnew_b[oc % 2][:]), "st", 16)

                def emit_ffn_sub(L, s_idx):
                    in32, inbf = masters[1 + ((s_idx - 1) % 2)]
                    o32, obf = masters[1 + (s_idx % 2)]
                    with ExitStack() as fctx:
                        sbf = lambda name, shape, dt: fctx.enter_context(
                            nc.sbuf_tensor(f"L{L}f_{name}", shape, dt))
                        W1_sb = sbf("w1", [128, 8 * W1O], BF16)
                        w2p = [sbf(f"w2p{i}", [128, 1024], BF16) for i in range(2)]
                        xn_b = sbf("xn", [128, 8 * 512], BF16)
                        h_sb = sbf("h", [128, 22 * 512], BF16)

                        @sec("sync")
                        def _(sync):
                            kb.wait(sync, "sync", "wt", precast_done)
                            kb.wait(sync, "sync", "pe")
                            kb.wait(sync, "sync", "dve")
                            kb.wait(sync, "sync", "act")
                            kb.wait(sync, "sync", "tr")
                            kb.wait(sync, "sync", "st")
                            for cch in range(NCH):
                                kb.bump(sync.dma_start(
                                    out=W1_sb[:, cch * W1O:(cch + 1) * W1O],
                                    in_=hb["W1B"][L * D + cch * 128: L * D + (cch + 1) * 128, :]),
                                    "wt", 16)
                            pm["w1_ld"] = kb.val("wt")

                        for tile in range(NTILE):
                            b, tl = tile // 4, tile % 4
                            g0 = tile * TT

                            @sec("sync")
                            def _(sync, g0=g0, tl=tl):
                                kb.wait(sync, "sync", "pe", pm.get("ln_mm", 0))
                                kb.wait(sync, "sync", "dve", pm.get("xn_done", 0))
                                kb.wait(sync, "sync", "act", pm.get("sq_act", 0))
                                if tl == 0:
                                    pm["xs_ld"] = kb.bump(sync.dma_start(
                                        out=xs_bf[:].rearrange("p (c t) -> p c t", c=8)[:, :, 1:513],
                                        in_=hb[inbf].rearrange("(c p) t -> p c t", p=128)[:, :, g0:g0 + TT]),
                                        "ld", 16)
                                else:
                                    pm["xs_ld"] = kb.bump(sync.dma_start(
                                        out=xs_bf[:].rearrange("p (c t) -> p c t", c=8),
                                        in_=hb[inbf].rearrange("(c p) t -> p c t", p=128)[:, :, g0 - 1:g0 + TT]),
                                        "ld", 16)

                            if tl == 0:
                                @sec("vector")
                                def _(vector):
                                    kb.wait(vector, "vector", "ld", pm["xs_ld"])
                                    pm["xs_dve"] = kb.bump(vector.memset(
                                        xs_bf[:].rearrange("p (c t) -> p c t", c=8)[:, :, 0:1], 0.0),
                                        "dve", 1)

                            def xs_slice(kc):
                                if kc < 4:
                                    return xs_bf[:, kc * 513 + 1: kc * 513 + 513]
                                return xs_bf[:, kc * 513: kc * 513 + 512]

                            for kc in range(8):
                                @sec("scalar")
                                def _(scalar, kc=kc, tl=tl):
                                    kb.wait(scalar, "scalar", "ld", pm["xs_ld"])
                                    kb.wait(scalar, "scalar", "pe", pm.get("ssq_mm", 0))
                                    pm["sq_act"] = kb.bump(scalar.activation(
                                        scr_b[:, 0:512], xs_slice(kc), AF.Square), "act", 1)

                                @sec("tensor")
                                def _(tensor, kc=kc, tl=tl):
                                    if tl == 0:
                                        kb.wait(tensor, "tensor", "dve", pm.get("xs_dve", 0))
                                    kb.wait(tensor, "tensor", "act", pm["sq_act"])
                                    if kc == 0:
                                        bank_wait(tensor, 3)
                                        bank_wait(tensor, 4)
                                    tensor.matmul(pb[3][0:1, 0:512], sbc["ones_col_bf"][:],
                                                  xs_slice(kc), start=(kc == 0), stop=(kc == 7))
                                    pm["ssq_mm"] = kb.bump(tensor.matmul(
                                        pb[4][0:1, 0:512], sbc["ones_col_bf"][:],
                                        scr_b[:, 0:512], start=(kc == 0), stop=(kc == 7)), "pe", 1)
                            pm["ln_mm"] = pm["ssq_mm"]

                            @sec("vector")
                            def _(vector):
                                kb.wait(vector, "vector", "pe", pm["ssq_mm"])
                                kb.wait(vector, "vector", "dve")
                                vector.tensor_scalar(rowa[0:1, 0:512], pb[3][0:1, 0:512],
                                                     1.0 / 1024.0, None, ALU.mult)
                                kb.fence(vector, "vector", vector.tensor_scalar(
                                    rowa[0:1, 512:1024], pb[4][0:1, 0:512], 1.0 / 1024.0, None,
                                    ALU.mult), "dve")
                                kb.fence(vector, "vector", vector.tensor_tensor(
                                    rowb[0:1, 0:512], rowa[0:1, 0:512], rowa[0:1, 0:512],
                                    ALU.mult), "dve")
                                pm["var_done"] = kb.bump(vector.tensor_tensor(
                                    rowb[0:1, 512:1024], rowa[0:1, 512:1024], rowb[0:1, 0:512],
                                    ALU.subtract), "dve", 1)

                            @sec("scalar")
                            def _(scalar):
                                kb.wait(scalar, "scalar", "dve", pm["var_done"])
                                pm["ln_sd"] = kb.bump(scalar.activation(
                                    rowb[0:1, 0:512], rowb[0:1, 512:1024], AF.Sqrt,
                                    bias=CV[0:1, 0:1]), "act", 1)

                            @sec("vector")
                            def _(vector):
                                kb.wait(vector, "vector", "act", pm["ln_sd"])
                                pm["ln_r"] = kb.bump(vector.reciprocal(rowb[0:1, 512:1024],
                                                                       rowb[0:1, 0:512]), "dve", 1)

                            @sec("tensor")
                            def _(tensor):
                                kb.wait(tensor, "tensor", "dve", pm["ln_r"])
                                tensor.matmul(pb[3][:, 0:512], sbc["ones_row_f32"][0:1, :],
                                              rowa[0:1, 0:512], start=True, stop=True)
                                pm["ln_bc"] = kb.bump(tensor.matmul(
                                    pb[4][:, 0:512], sbc["ones_row_f32"][0:1, :],
                                    rowb[0:1, 512:1024], start=True, stop=True), "pe", 1)

                            for kc in range(8):
                                @sec("vector")
                                def _(vector, kc=kc):
                                    kb.wait(vector, "vector", "pe", pm["ln_bc"])
                                    kb.fence(vector, "vector", vector.tensor_tensor(
                                        scr_f[:, 0:512], xs_slice(kc), pb[3][:, 0:512],
                                        ALU.subtract), "dve")
                                    kb.fence(vector, "vector", vector.scalar_tensor_tensor(
                                        scr_f[:, 512:1024], scr_f[:, 0:512],
                                        lng_cols[:, kc:kc + 1], pb[4][:, 0:512],
                                        ALU.mult, ALU.mult), "dve")
                                    v = kb.bump(vector.tensor_scalar(
                                        xn_b[:, kc * 512:(kc + 1) * 512], scr_f[:, 512:1024],
                                        lnb_cols[:, kc:kc + 1], None, ALU.add), "dve", 1)
                                    pm["xn_done"] = v
                            pm["bank"][3] = ("dve", pm["xn_done"])
                            pm["bank"][4] = ("dve", pm["xn_done"])

                            for ci, (off, ln) in enumerate(W2CH):
                                pa = pb[(2 * ci) % 4]
                                pg = pb[(2 * ci) % 4 + 1]

                                @sec("tensor")
                                def _(tensor, ci=ci, off=off, ln=ln, pa=pa, pg=pg):
                                    kb.wait(tensor, "tensor", "dve", pm["xn_done"])
                                    kb.wait(tensor, "tensor", "wt", pm["w1_ld"])
                                    bank_wait(tensor, (2 * ci) % 4)
                                    bank_wait(tensor, (2 * ci) % 4 + 1)
                                    for kc in range(8):
                                        tensor.matmul(pa[0:ln, 0:512],
                                                      W1_sb[:, kc * W1O + off: kc * W1O + off + ln],
                                                      xn_b[:, kc * 512:(kc + 1) * 512],
                                                      start=(kc == 0), stop=(kc == 7))
                                    for kc in range(8):
                                        ins = tensor.matmul(
                                            pg[0:ln, 0:512],
                                            W1_sb[:, kc * W1O + HALF1 + off: kc * W1O + HALF1 + off + ln],
                                            xn_b[:, kc * 512:(kc + 1) * 512],
                                            start=(kc == 0), stop=(kc == 7))
                                    pm["w1_mm"] = kb.bump(ins, "pe", 1)

                                @sec("scalar")
                                def _(scalar, ci=ci, ln=ln, pg=pg):
                                    kb.wait(scalar, "scalar", "pe", pm["w1_mm"])
                                    kb.wait(scalar, "scalar", "dve", pm.get("h_done", 0))
                                    pm["gelu"] = kb.bump(scalar.activation(
                                        scr_b[0:ln, 0:512], pg[0:ln, 0:512], AF.Gelu,
                                        bias=b1_cols[0:ln, 22 + ci:23 + ci]), "act", 1)

                                @sec("vector")
                                def _(vector, ci=ci, ln=ln, pa=pa, pg=pg):
                                    kb.wait(vector, "vector", "act", pm["gelu"])
                                    kb.fence(vector, "vector", vector.tensor_scalar(
                                        scr_b[0:ln, 512:1024], pa[0:ln, 0:512],
                                        b1_cols[0:ln, ci:ci + 1], None, ALU.add), "dve")
                                    v = kb.bump(vector.tensor_tensor(
                                        h_sb[0:ln, ci * 512:(ci + 1) * 512],
                                        scr_b[0:ln, 512:1024], scr_b[0:ln, 0:512], ALU.mult),
                                        "dve", 1)
                                    pm["h_done"] = v
                                    pm["bank"][(2 * ci) % 4] = ("dve", v)
                                    pm["bank"][(2 * ci) % 4 + 1] = ("dve", v)

                            for ci, (off, ln) in enumerate(W2CH):
                                @sec("sync")
                                def _(sync, ci=ci, off=off, ln=ln, L=L):
                                    kb.wait(sync, "sync", "pe", pm.get("w2_mm_%d" % (ci % 2), 0))
                                    pm["w2_ld"] = kb.bump(sync.dma_start(
                                        out=w2p[ci % 2][0:ln, :],
                                        in_=hb["W2B"][L * HALF1 + off: L * HALF1 + off + ln, :]),
                                        "wt", 16)

                                @sec("tensor")
                                def _(tensor, ci=ci, off=off, ln=ln):
                                    kb.wait(tensor, "tensor", "wt", pm["w2_ld"])
                                    kb.wait(tensor, "tensor", "dve", pm["h_done"])
                                    if ci == 0:
                                        for i2 in range(8):
                                            bank_wait(tensor, i2)
                                    for oc in range(8):
                                        ins = tensor.matmul(
                                            pb[oc][:, 0:512],
                                            w2p[ci % 2][0:ln, oc * 128:(oc + 1) * 128],
                                            h_sb[0:ln, ci * 512:(ci + 1) * 512],
                                            start=(ci == 0), stop=(ci == 21))
                                    pm["w2_mm_%d" % (ci % 2)] = kb.bump(ins, "pe", 1)
                                    if ci == 21:
                                        pm["w2_mm"] = pm["w2_mm_%d" % (ci % 2)]

                            for oc in range(8):
                                @sec("sync")
                                def _(sync, oc=oc, g0=g0):
                                    kb.wait(sync, "sync", "dve", pm["xnew_free"][oc % 2])
                                    pm["res_ld"][oc % 2] = kb.bump(sync.dma_start(
                                        out=res_f[oc % 2][:],
                                        in_=hb[in32][oc * 128:(oc + 1) * 128, g0:g0 + TT]), "ld", 16)

                                @sec("vector")
                                def _(vector, oc=oc):
                                    kb.wait(vector, "vector", "pe", pm["w2_mm"])
                                    kb.wait(vector, "vector", "ld", pm["res_ld"][oc % 2])
                                    kb.wait(vector, "vector", "st", pm.get("xst_%d" % (oc % 2), 0))
                                    v = kb.bump(vector.scalar_tensor_tensor(
                                        xnew_f[oc % 2][:], res_f[oc % 2][:],
                                        b2_cols[:, oc:oc + 1], pb[oc][:, 0:512],
                                        ALU.add, ALU.add), "dve", 1)
                                    pm["bank"][oc] = ("dve", v)
                                    kb.wait(vector, "vector", "dve", v)
                                    pm["xnew_dve"] = kb.bump(vector.tensor_copy(
                                        xnew_b[oc % 2][:], xnew_f[oc % 2][:]), "dve", 1)
                                    pm["xnew_free"][oc % 2] = pm["xnew_dve"]

                                @sec("sync")
                                def _(sync, oc=oc, g0=g0):
                                    kb.wait(sync, "sync", "dve", pm["xnew_dve"])
                                    kb.bump(sync.dma_start(
                                        out=hb[o32][oc * 128:(oc + 1) * 128, g0:g0 + TT],
                                        in_=xnew_f[oc % 2][:]), "st", 16)
                                    pm["xst_%d" % (oc % 2)] = kb.bump(sync.dma_start(
                                        out=hb[obf][oc * 128:(oc + 1) * 128, g0:g0 + TT],
                                        in_=xnew_b[oc % 2][:]), "st", 16)

                for s_idx in range(n_sub):
                    L = s_idx // 2
                    if s_idx % 2 == 0:
                        emit_attn_sub(L, s_idx)
                    else:
                        emit_ffn_sub(L, s_idx)

        if "p3" in phases:
            fin32 = masters[1 + ((2 * depth - 1) % 2)][0]
            with ExitStack() as pctx:
                sb3 = lambda name, shape, dt: pctx.enter_context(nc.sbuf_tensor(name, shape, dt))
                xf_t = sb3("p3_xf", [128, 8 * 512], F32)
                s_t = sb3("p3_s", [128, 8 * 512], F32)
                x_o = sb3("p3_xo", [128, 4 * 1024], F32)
                out_sb = sb3("p3_out", [128, 4 * 1024], F32)
                sg_t = sb3("p3_sg", [128, 8 * 512], F32)
                fm = {"ld": 0, "sg": 0, "tp": [0] * 8, "tt": [0] * 8, "st": 0}

                for tile in range(NTILE):
                    g0 = tile * TT

                    @sec("sync")
                    def _(sync, g0=g0, tile=tile):
                        if tile == 0:
                            for s in ["pe", "dve", "act", "pool", "ld", "wt", "tr"]:
                                kb.wait(sync, "sync", s)
                        else:
                            kb.wait(sync, "sync", "dve", fm["tt_last"])
                            kb.wait(sync, "sync", "act", fm["silu"])
                        kb.wait(sync, "sync", "st")
                        kb.bump(sync.dma_start(
                            out=xf_t[:].rearrange("p (c t) -> p c t", c=8),
                            in_=hb[fin32].rearrange("(c p) t -> p c t", p=128)[:, :, g0:g0 + TT]),
                            "ld", 16)
                        kb.bump(sync.dma_start(
                            out=s_t[:].rearrange("p (c t) -> p c t", c=8),
                            in_=hb["S32"].rearrange("(c p) t -> p c t", p=128)[:, :, g0:g0 + TT]),
                            "ld", 16)
                        fm["ld"] = kb.bump(sync.dma_start(
                            out=x_o[:].rearrange("p (q d) -> p q d", q=4),
                            in_=xin.rearrange("(q p) d -> p q d", p=128)[:, 4 * tile:4 * tile + 4, :]),
                            "ld", 16)

                    @sec("scalar")
                    def _(scalar, tile=tile):
                        kb.wait(scalar, "scalar", "ld", fm["ld"])
                        kb.wait(scalar, "scalar", "dve", fm.get("sg_done", 0))
                        for cch in range(8):
                            ins = scalar.activation(sg_t[:, cch * 512:(cch + 1) * 512],
                                                    xf_t[:, cch * 512:(cch + 1) * 512], AF.Silu)
                        fm["silu"] = kb.bump(ins, "act", 1)

                    @sec("vector")
                    def _(vector, tile=tile):
                        kb.wait(vector, "vector", "act", fm["silu"])
                        kb.wait(vector, "vector", "pe", fm.get("tp_last", 0))
                        for cch in range(8):
                            ins = vector.tensor_tensor(xf_t[:, cch * 512:(cch + 1) * 512],
                                                       sg_t[:, cch * 512:(cch + 1) * 512],
                                                       s_t[:, cch * 512:(cch + 1) * 512], ALU.mult)
                        fm["sg_done"] = kb.bump(ins, "dve", 1)

                    for qb in range(4):
                        @sec("tensor")
                        def _(tensor, qb=qb, tile=tile):
                            kb.wait(tensor, "tensor", "dve", fm["sg_done"])
                            for cch in range(8):
                                kb.wait(tensor, "tensor", "dve", fm["tt"][cch])
                                ins = tensor.transpose(pb[cch][:, 0:128],
                                                       xf_t[:, cch * 512 + qb * 128: cch * 512 + (qb + 1) * 128],
                                                       sbc["ident_f32"][:])
                                fm["tp"][cch] = kb.bump(ins, "pe", 1)
                            fm["tp_last"] = fm["tp"][7]

                        @sec("vector")
                        def _(vector, qb=qb, tile=tile):
                            kb.wait(vector, "vector", "st", fm["st"])
                            for cch in range(8):
                                kb.wait(vector, "vector", "pe", fm["tp"][cch])
                                ins = vector.tensor_tensor(
                                    out_sb[:, qb * 1024 + cch * 128: qb * 1024 + (cch + 1) * 128],
                                    pb[cch][:, 0:128],
                                    x_o[:, qb * 1024 + cch * 128: qb * 1024 + (cch + 1) * 128],
                                    ALU.add)
                                fm["tt"][cch] = kb.bump(ins, "dve", 1)
                            fm["tt_last"] = fm["tt"][7]

                    @sec("sync")
                    def _(sync, g0=g0, tile=tile):
                        kb.wait(sync, "sync", "dve", fm["tt_last"])
                        fm["st"] = kb.bump(sync.dma_start(
                            out=out_ap.rearrange("(q p) d -> p q d", p=128)[:, 4 * tile:4 * tile + 4, :],
                            in_=out_sb[:].rearrange("p (q d) -> p q d", q=4)), "st", 16)

        @sec("sync")
        def _(sync):
            kb.wait(sync, "sync", "st")
            kb.wait(sync, "sync", "wt")
            kb.wait(sync, "sync", "act")
            kb.wait(sync, "sync", "dve")
            kb.wait(sync, "sync", "pe")
            for dn in debug_names:
                kb.bump(sync.dma_start(out=dbg[dn], in_=hb[dn]), "st", 16)
            kb.wait(sync, "sync", "st")

    return nc, consts


def make_in_maps(inputs, consts):
    x = np.ascontiguousarray(inputs["L0_attn_hidden_states"])
    in_maps = []
    ncores = x.shape[0] // 2
    for core in range(ncores):
        m = {"x": np.ascontiguousarray(x[2 * core:2 * core + 2].reshape(NTOK, D))}
        m.update({
            "aWq": inputs["L0_attn_Wq"], "aWk": inputs["L0_attn_Wk"], "aWv": inputs["L0_attn_Wv"],
            "aWg": inputs["L0_attn_Wg"], "aWo": inputs["L0_attn_Wo"],
            "acq": inputs["L0_attn_conv_q"], "ack": inputs["L0_attn_conv_k"],
            "acv": inputs["L0_attn_conv_v"],
            "aW0": np.ascontiguousarray(inputs["L0_attn_W_init"].reshape(16 * DK, DK)),
            "agam": inputs["L0_attn_ln_gamma"], "abet": inputs["L0_attn_ln_beta"],
            "fWqkv": inputs["L0_ffn_Wqkv"], "fqs": inputs["L0_ffn_q_scale"],
            "fks": inputs["L0_ffn_k_scale"], "fWo": inputs["L0_ffn_Wo"],
            "flng": inputs["L0_ffn_ln_g"], "flnb": inputs["L0_ffn_ln_b"],
            "fW1": inputs["L0_ffn_W1"], "fb1": inputs["L0_ffn_b1"],
            "fW2": inputs["L0_ffn_W2"], "fb2": inputs["L0_ffn_b2"],
        })
        for nm in CONST_SPECS:
            m["c_" + nm] = consts[nm]
        in_maps.append(m)
    return in_maps


def kernel(**inputs):
    nc, consts = build_program()
    in_maps = make_in_maps(inputs, consts)
    res = run_bass_kernel_spmd(nc, in_maps, core_ids=list(range(8)))
    outs = [r["out"].reshape(2, SEQ, D) for r in res.results]
    return np.concatenate(outs, axis=0)


# version 04
# speedup vs baseline: 1.2268x; 1.2268x over previous
import numpy as np
import ml_dtypes
import concourse.bass as bass
import concourse.mybir as mybir
from concourse.bass_utils import run_bass_kernel_spmd
from contextlib import ExitStack

F32 = mybir.dt.float32
BF16 = mybir.dt.bfloat16
AF = mybir.ActivationFunctionType
ALU = mybir.AluOpType
AX = mybir.AxisListType
bfdt = ml_dtypes.bfloat16

P = 128
D = 1024
NCH = 8
TT = 512
NTILE = 8
SEQ = 2048
B = 2
HEADS = 16
DK = 64
DEPTH = 6
W1O = 5460
HALF1 = 2730
NTOK = B * SEQ
EXP_BIAS = -10.0

def chunks_of(total, step=128):
    out = []
    off = 0
    while off < total:
        out.append((off, min(step, total - off)))
        off += step
    return out

W2CH = chunks_of(HALF1)


def host_constants():
    c = {}
    c["ident_f32"] = np.eye(128, dtype=np.float32)
    c["negI_bf"] = (-1e30 * np.eye(128)).astype(bfdt)
    kk = np.arange(128)[:, None]
    ii = np.arange(128)[None, :]
    m0 = (kk > ii).astype(np.float32)
    m2 = (kk < ii).astype(np.float32)
    c["maskU_bf"] = np.concatenate([m0, np.zeros((128, 128), np.float32), m2], 1).astype(bfdt)
    sub_k = kk // 16
    sub_q = ii // 16
    mt = ((sub_k < sub_q) | ((sub_k == sub_q) & (kk <= ii))).astype(np.float32)
    c["maskTT_bf"] = mt.astype(bfdt)
    c["ones_col_bf"] = np.ones((128, 1), dtype=bfdt)
    c["ones_row_bf"] = np.ones((1, 128), dtype=bfdt)
    c["ones_row_f32"] = np.ones((1, 128), dtype=np.float32)
    sel2 = np.zeros((128, 2), np.float32)
    sel2[:64, 0] = 1.0
    sel2[64:, 1] = 1.0
    c["sel2_bf"] = sel2.astype(bfdt)
    c["selT_bf"] = sel2.T.copy().astype(bfdt)
    r = np.zeros((64, 64), np.float32)
    for d in range(32):
        r[d + 32, d] = -1.0
        r[d, d + 32] = 1.0
    R = np.zeros((128, 128), np.float32)
    R[:64, :64] = r
    R[64:, 64:] = r
    c["R2T_bf"] = R.astype(bfdt)
    half = np.arange(0, DK, 2, dtype=np.float64)
    inv_freq = 1.0 / 10000.0 ** (half / DK)
    t = np.arange(SEQ, dtype=np.float64)
    fr32 = t[:, None] * inv_freq[None, :]
    freqs = np.concatenate([fr32, fr32], 1)
    base_scale = (half + 0.4 * DK) / (1.4 * DK)
    power = (t - SEQ // 2) / (256 // 2)
    sc32 = base_scale[None, :] ** power[:, None]
    scale = np.concatenate([sc32, sc32], 1)
    cos = np.cos(freqs)
    sin = np.sin(freqs)
    A = (cos * scale).T
    Bt = (sin * scale).T
    C = (cos / scale).T
    Dt = (sin / scale).T
    dup = lambda m: np.concatenate([m, m], 0).astype(bfdt)
    c["rotA"] = dup(A)
    c["rotB"] = dup(Bt)
    c["rotC"] = dup(C)
    c["rotD"] = dup(Dt)
    cv = np.zeros((128, 8), np.float32)
    cv[:, 0] = 1e-5
    cv[:, 1] = 1e-12
    cv[:, 2] = EXP_BIAS
    cv[:, 3] = 1.0 / 64.0
    cv[:, 4] = 1.0 / 1024.0
    c["cvec"] = cv
    return c

CONST_SPECS = {
    "ident_f32": F32, "negI_bf": BF16, "maskU_bf": BF16, "maskTT_bf": BF16,
    "ones_col_bf": BF16, "ones_row_bf": BF16, "ones_row_f32": F32,
    "sel2_bf": BF16, "selT_bf": BF16, "R2T_bf": BF16,
    "rotA": BF16, "rotB": BF16, "rotC": BF16, "rotD": BF16, "cvec": F32,
}

DBG_SHAPES = {"XN32": ([D, NTOK], F32), "XNBF": ([D, NTOK], BF16),
              "S32": ([D, NTOK], F32), "XA32": ([D, NTOK], F32),
              "XB32": ([D, NTOK], F32), "XBBF": ([D, NTOK], BF16),
              "QD": ([D, NTOK], BF16), "KD": ([D, NTOK], BF16),
              "VD": ([D, NTOK], BF16), "GD": ([D, NTOK], BF16),
              "OD": ([NTOK, D], BF16),
              "Q2D": ([D, NTOK], BF16), "K2D": ([D, NTOK], BF16),
              "V2D": ([D, NTOK], BF16), "OGD": ([D, NTOK], BF16),
              "WQKVB": ([DEPTH * D, 3 * D], BF16), "W1B": ([DEPTH * D, W1O], BF16)}



class KB:

    def __init__(self, nc):
        self.nc = nc
        self.sems = {}
        self.cnt = {}
        self.waited = {}

    def bump(self, instr, sem_name, k=1):
        instr.then_inc(self.sems[sem_name], k)
        self.cnt[sem_name] += k
        return self.cnt[sem_name]

    def val(self, sem_name):
        return self.cnt[sem_name]

    def fence(self, eng, eng_name, instr, sem_name):
        v = self.bump(instr, sem_name, 1)
        self.wait(eng, eng_name, sem_name, v)

    DMA_SEMS = ("ld", "wt", "st", "tr")

    def wait(self, eng, eng_name, sem_name, v=None):
        if v is None or sem_name in KB.DMA_SEMS:
            v = self.cnt[sem_name]
        if v <= 0:
            return
        key = (eng_name, sem_name)
        if self.waited.get(key, 0) >= v:
            return
        eng.wait_ge(self.sems[sem_name], v)
        self.waited[key] = v


def build_program(debug_names=(), phases=("p0", "p1", "p2", "p3"), depth=DEPTH, p2_sub=None, paranoid=False):
    debug_names = set(debug_names)
    nc = bass.Bass("TRN2", target_bir_lowering=False)
    consts = host_constants()

    xin = nc.dram_tensor("x", [NTOK, D], F32, kind="ExternalInput").ap()
    out_ap = nc.dram_tensor("out", [NTOK, D], F32, kind="ExternalOutput").ap()
    wt = {}
    for nm, shape in [
        ("aWq", [D, D]), ("aWk", [D, D]), ("aWv", [D, D]), ("aWg", [D, D]), ("aWo", [D, D]),
        ("acq", [D, 4]), ("ack", [D, 4]), ("acv", [D, 4]),
        ("aW0", [16 * DK, DK]), ("agam", [DK]), ("abet", [DK]),
        ("fWqkv", [DEPTH, D, 3 * D]), ("fqs", [DEPTH, DK]), ("fks", [DEPTH, DK]),
        ("fWo", [DEPTH, D, D]), ("flng", [DEPTH, D]), ("flnb", [DEPTH, D]),
        ("fW1", [DEPTH, D, W1O]), ("fb1", [DEPTH, W1O]), ("fW2", [DEPTH, HALF1, D]),
        ("fb2", [DEPTH, D]),
    ]:
        wt[nm] = nc.dram_tensor(nm, shape, F32, kind="ExternalInput").ap()
    cst_d = {}
    for nm, dt in CONST_SPECS.items():
        cst_d[nm] = nc.dram_tensor("c_" + nm, list(consts[nm].shape), dt, kind="ExternalInput").ap()

    hb = {}
    for nm, dt in [("XN32", F32), ("XNBF", BF16), ("XA32", F32), ("XABF", BF16),
                   ("XB32", F32), ("XBBF", BF16), ("S32", F32),
                   ("QD", BF16), ("KD", BF16), ("VD", BF16), ("GD", BF16)]:
        hb[nm] = nc.dram_tensor(nm, [D, NTOK], dt, kind="Internal").ap()
    hb["OD"] = nc.dram_tensor("OD", [NTOK, D], BF16, kind="Internal").ap()
    for nm in ["Q2D", "K2D", "V2D", "OGD"]:
        hb[nm] = nc.dram_tensor(nm, [D, NTOK], BF16, kind="Internal").ap()
    hb["WQKVB"] = nc.dram_tensor("WQKVB", [DEPTH * D, 3 * D], BF16, kind="Internal").ap()
    hb["WOB"] = nc.dram_tensor("WOB", [DEPTH * D, D], BF16, kind="Internal").ap()
    hb["W1B"] = nc.dram_tensor("W1B", [DEPTH * D, W1O], BF16, kind="Internal").ap()
    hb["W2B"] = nc.dram_tensor("W2B", [DEPTH * HALF1, D], BF16, kind="Internal").ap()

    dbg = {}
    for dn in debug_names:
        shp, dt = DBG_SHAPES[dn]
        dbg[dn] = nc.dram_tensor("dbg_" + dn, shp, dt, kind="ExternalOutput").ap()

    kb = KB(nc)

    with ExitStack() as ctx:
        sbt = lambda name, shape, dt: ctx.enter_context(nc.sbuf_tensor(name, shape, dt))
        for s in ["ld", "wt", "tr", "st", "pe", "dve", "act", "pool"]:
            kb.sems[s] = ctx.enter_context(nc.semaphore("s_" + s))
            kb.cnt[s] = 0

        sbc = {}
        for nm, dt in CONST_SPECS.items():
            sbc[nm] = sbt("k_" + nm, list(consts[nm].shape), dt)
        gam_bc = sbt("gam_bc", [128, 64], F32)
        bet_bc = sbt("bet_bc", [128, 64], F32)
        convw = sbt("convw", [128, 3 * 32], F32)
        convp3 = sbt("convp3", [128, 3 * 8], F32)
        W0_sb = sbt("W0_sb", [128, 8 * 64], F32)
        gb_row = sbt("gb_row", [1, 128], F32)
        qs_col = sbt("qs_col", [128, 1], F32)
        ks_col = sbt("ks_col", [128, 1], F32)
        b1_cols = sbt("b1_cols", [128, 44], F32)
        b2_cols = sbt("b2_cols", [128, 8], F32)
        lng_cols = sbt("lng_cols", [128, 8], F32)
        lnb_cols = sbt("lnb_cols", [128, 8], F32)

        pb = [ctx.enter_context(nc.psum_tensor(f"pb{i}", [128, 512], F32)) for i in range(8)]

        SET_NLEXP, SET_GELU, SET_SILU = 6, 10, 17

        def preload_set(idx):
            return nc.scalar.add_instruction(
                mybir.InstLoadActFuncSet(
                    name=nc.get_next_instruction_name(),
                    act_func_set_id=idx, ins=[], outs=[]))

        block = ctx.enter_context(nc.Block())

        def sec(eng_name):
            raw = getattr(block, eng_name)
            pset = set(paranoid.split(",")) if isinstance(paranoid, str) and paranoid else (
                {"sync", "tensor", "vector", "scalar", "gpsimd"} if paranoid is True else set())
            if eng_name not in pset:
                return raw

            def deco(fn):
                def wrapped(eng):
                    for s in ["pe", "dve", "act", "pool", "ld", "wt", "st", "tr"]:
                        kb.wait(eng, eng_name, s)
                    return fn(eng)
                return raw(wrapped)
            return deco

        CV = sbc["cvec"]

        @sec("sync")
        def _(sync):
            for nm in CONST_SPECS:
                kb.bump(sync.dma_start(out=sbc[nm][:], in_=cst_d[nm]), "ld", 16)
            for pi, nm in enumerate(["acq", "ack", "acv"]):
                for cch in range(NCH):
                    kb.bump(sync.dma_start(
                        out=convw[:, pi * 32 + 4 * cch: pi * 32 + 4 * cch + 4],
                        in_=wt[nm][cch * 128:(cch + 1) * 128, :]), "ld", 16)
            for cch in range(NCH):
                kb.bump(sync.dma_start(
                    out=W0_sb[:, cch * 64:(cch + 1) * 64],
                    in_=wt["aW0"][cch * 128:(cch + 1) * 128, :]), "ld", 16)
            kb.bump(sync.dma_start(out=gb_row[0:1, 0:64], in_=wt["agam"][None, :]), "ld", 16)
            kb.bump(sync.dma_start(out=gb_row[0:1, 64:128], in_=wt["abet"][None, :]), "ld", 16)

        @sec("vector")
        def _(vector):
            kb.wait(vector, "vector", "ld")
            for pi in range(3):
                src = convw[:, pi * 32 + 3: (pi + 1) * 32: 4]
                kb.bump(vector.tensor_scalar(
                    convp3[:, pi * 8:(pi + 1) * 8], src, 1.0, None, ALU.add), "dve", 1)

        @sec("tensor")
        def _(tensor):
            kb.wait(tensor, "tensor", "ld")
            kb.bump(tensor.matmul(pb[0][:, 0:64], sbc["ones_row_f32"][0:1, :],
                                  gb_row[0:1, 0:64], start=True, stop=True), "pe", 1)
            kb.bump(tensor.matmul(pb[1][:, 0:64], sbc["ones_row_f32"][0:1, :],
                                  gb_row[0:1, 64:128], start=True, stop=True), "pe", 1)

        @sec("scalar")
        def _(scalar):
            kb.wait(scalar, "scalar", "pe")
            scalar.copy(gam_bc[:], pb[0][:, 0:64])
            kb.bump(scalar.copy(bet_bc[:], pb[1][:, 0:64]), "act", 1)

        if "p0" in phases:
            with ExitStack() as pctx:
                sb0 = lambda name, shape, dt: pctx.enter_context(nc.sbuf_tensor(name, shape, dt))
                xt = [sb0(f"p0_xt{i}", [128, 1024], F32) for i in range(2)]
                stat = sb0("p0_stat", [128, 16], F32)
                xn_f = [sb0(f"p0_xn{i}", [128, 1024], F32) for i in range(2)]
                xnT_f = [sb0(f"p0_xnT{i}", [128, 1024], F32) for i in range(2)]
                xnT_b = [sb0(f"p0_xnTb{i}", [128, 1024], BF16) for i in range(2)]

                NT0 = NTOK // 128
                dve_done = [0] * NT0
                pe_done = [0] * NT0
                ev_done = [0] * NT0
                st_done = [0] * NT0
                init_act = kb.val("act")

                for i in range(NT0):
                    @sec("sync")
                    def _(sync, i=i):
                        if i >= 2:
                            kb.wait(sync, "sync", "dve", dve_done[i - 2])
                        kb.bump(sync.dma_start(out=xt[i % 2][:],
                                               in_=xin[i * 128:(i + 1) * 128, :]), "ld", 16)

                    @sec("vector")
                    def _(vector, i=i):
                        kb.wait(vector, "vector", "ld")
                        t = xt[i % 2]
                        vector.bn_stats(stat[:, 4:10], t[:, 0:512])
                        kb.fence(vector, "vector",
                                 vector.bn_stats(stat[:, 10:16], t[:, 512:1024]), "dve")
                        kb.bump(vector.bn_aggr(stat[:, 0:2], stat[:, 4:16]), "dve", 1)

                    @sec("scalar")
                    def _(scalar, i=i):
                        kb.wait(scalar, "scalar", "dve")
                        kb.fence(scalar, "scalar",
                                 scalar.activation(stat[:, 2:3], stat[:, 1:2], AF.Ln,
                                                   bias=CV[:, 0:1]), "act")
                        kb.bump(scalar.activation(stat[:, 3:4], stat[:, 2:3], AF.Exp,
                                                  scale=-0.5), "act", 1)

                    @sec("vector")
                    def _(vector, i=i):
                        kb.wait(vector, "vector", "act")
                        if i >= 2:
                            kb.wait(vector, "vector", "pe", pe_done[i - 2])
                        dve_done[i] = kb.bump(
                            vector.tensor_scalar(xn_f[i % 2][:], xt[i % 2][:], stat[:, 0:1],
                                                 stat[:, 3:4], ALU.subtract, ALU.mult), "dve", 1)

                    @sec("tensor")
                    def _(tensor, i=i):
                        kb.wait(tensor, "tensor", "dve", dve_done[i])
                        kb.wait(tensor, "tensor", "act",
                                ev_done[i - 1] if i >= 1 else init_act)
                        for cch in range(8):
                            ins = tensor.transpose(pb[cch][:, 0:128],
                                                   xn_f[i % 2][:, cch * 128:(cch + 1) * 128],
                                                   sbc["ident_f32"][:])
                        pe_done[i] = kb.bump(ins, "pe", 1)

                    @sec("scalar")
                    def _(scalar, i=i):
                        kb.wait(scalar, "scalar", "pe", pe_done[i])
                        if i >= 2:
                            kb.wait(scalar, "scalar", "st", st_done[i - 2])
                        for cch in range(8):
                            scalar.copy(xnT_f[i % 2][:, cch * 128:(cch + 1) * 128],
                                        pb[cch][:, 0:128])
                            ins = scalar.copy(xnT_b[i % 2][:, cch * 128:(cch + 1) * 128],
                                              pb[cch][:, 0:128])
                        ev_done[i] = kb.bump(ins, "act", 1)

                    @sec("sync")
                    def _(sync, i=i):
                        kb.wait(sync, "sync", "act", ev_done[i])
                        kb.bump(sync.dma_start(
                            out=hb["XN32"].rearrange("(c p) t -> p c t", p=128)[:, :, i * 128:(i + 1) * 128],
                            in_=xnT_f[i % 2][:].rearrange("p (c t) -> p c t", c=8)), "st", 16)
                        st_done[i] = kb.bump(sync.dma_start(
                            out=hb["XNBF"].rearrange("(c p) t -> p c t", p=128)[:, :, i * 128:(i + 1) * 128],
                            in_=xnT_b[i % 2][:].rearrange("p (c t) -> p c t", c=8)), "st", 16)

        if "p1" in phases:
            with ExitStack() as pctx:
                sb1 = lambda name, shape, dt: pctx.enter_context(nc.sbuf_tensor(name, shape, dt))
                WT = {w: sb1(f"p1_WT_{w}", [128, 8 * 1024], BF16) for w in "qkvgo"}
                with ExitStack() as wctx:
                    stage_f = [wctx.enter_context(nc.sbuf_tensor(f"p1_ws{i}", [128, 1024], F32))
                               for i in range(2)]
                    stage_b = [wctx.enter_context(nc.sbuf_tensor(f"p1_wb{i}", [128, 1024], BF16))
                               for i in range(2)]
                    cast_mark = {}
                    tr_mark = {}
                    for wi, (w, src_nm) in enumerate(zip("qkvgo", ["aWq", "aWk", "aWv", "aWg", "aWo"])):
                        for r in range(8):
                            n = wi * 8 + r
                            @sec("sync")
                            def _(sync, src_nm=src_nm, r=r, n=n):
                                if n == 0:
                                    kb.wait(sync, "sync", "wt")
                                    kb.wait(sync, "sync", "pool")
                                if n >= 2:
                                    kb.wait(sync, "sync", "pool", cast_mark[n - 2])
                                kb.bump(sync.dma_start(out=stage_f[n % 2][:],
                                                       in_=wt[src_nm][r * 128:(r + 1) * 128, :]),
                                        "wt", 16)

                            @sec("gpsimd")
                            def _(gpsimd, n=n):
                                kb.wait(gpsimd, "gpsimd", "wt")
                                if n >= 2:
                                    kb.wait(gpsimd, "gpsimd", "tr", tr_mark[n - 2])
                                cast_mark[n] = kb.bump(
                                    gpsimd.tensor_copy(stage_b[n % 2][:], stage_f[n % 2][:]),
                                    "pool", 1)

                            @sec("sync")
                            def _(sync, w=w, r=r, n=n):
                                kb.wait(sync, "sync", "pool", cast_mark[n])
                                for j in range(8):
                                    sync.dma_start_transpose(
                                        WT[w][:, j * 1024 + r * 128: j * 1024 + (r + 1) * 128],
                                        stage_b[n % 2][:, j * 128:(j + 1) * 128]).then_inc(kb.sems["tr"], 16)
                                kb.cnt["tr"] += 16 * 8
                                tr_mark[n] = kb.val("tr")

                xnt = sb1("p1_xnt", [128, 8 * TT], BF16)
                q_bf = sb1("p1_q", [128, 8 * TT], BF16)
                k_bf = sb1("p1_k", [128, 8 * TT], BF16)
                v_bf = sb1("p1_v", [128, 8 * TT], BF16)
                g_bf = sb1("p1_g", [128, 8 * TT], BF16)
                u_ext = sb1("p1_uext", [128, 515], F32)
                ca_b = sb1("p1_ca", [128, 512], F32)
                cb_b = sb1("p1_cb", [128, 512], F32)
                qpre = sb1("p1_qpre", [128, 512], BF16)
                sqv = sb1("p1_sqv", [128, 512], BF16)
                sd_row = sb1("p1_sd", [2, 512], F32)
                rstd_bf = sb1("p1_rstd", [2, 512], BF16)
                tails = sb1("p1_tails", [128, 72], F32)
                ktok = [sb1(f"p1_ktok{i}", [128, 1024], BF16) for i in range(2)]
                vtok = [sb1(f"p1_vtok{i}", [128, 1024], BF16) for i in range(2)]
                gtok = [sb1(f"p1_gtok{i}", [128, 1024], BF16) for i in range(2)]
                W_bf = sb1("p1_Wbf", [128, 512], BF16)
                ptT = [sb1(f"p1_ptT{i}", [128, 128], BF16) for i in range(2)]
                mvstat = sb1("p1_mvstat", [128, 96], F32)
                mv = sb1("p1_mv", [128, 32], F32)
                rinv = sb1("p1_rinv", [128, 16], F32)
                sdv = sb1("p1_sdv", [128, 16], F32)
                t2_f = sb1("p1_t2", [128, 1024], F32)
                t_f = sb1("p1_t", [128, 1024], F32)
                o_all = sb1("p1_oall", [128, 4 * 1024], BF16)
                og = sb1("p1_og", [128, 4 * 1024], BF16)
                og_fm = sb1("p1_ogfm", [128, 8 * TT], BF16)
                s_stage = [sb1(f"p1_ss{i}", [128, 512], F32) for i in range(2)]

                mk = {"xnt_free": 0, "q_free": 0, "k_free": 0, "v_free": 0, "g_free": 0,
                      "bank_free": {2: ("pe", 0), 3: ("pe", 0), 4: ("pe", 0),
                                    5: ("dve", 0), 6: ("dve", 0), 7: ("act", 0)},
                      "uext_free": 0, "qpre_free": 0, "sqv_free": 0,
                      "wbf_evac": 0, "o_epi": {0: 0, 1: 0},
                      "tok_free": {0: 0, 1: 0}, "og_done": [0] * 4,
                      "sst_free": [0, 0]}

                rot_counter = [0]

                def B_chunk(pi, nm, oc, tile, first_of_seq):
                    bslot = 2 + (rot_counter[0] % 3)
                    rot_counter[0] += 1
                    bank = pb[bslot]

                    @sec("tensor")
                    def _(tensor):
                        kb.wait(tensor, "tensor", "ld", mk["xnt_ld"])
                        kb.wait(tensor, "tensor", "tr", mk["wt_ready"])
                        bs, bv = mk["bank_free"][bslot]
                        kb.wait(tensor, "tensor", bs, bv)
                        for kc in range(8):
                            ins = tensor.matmul(bank[:, 0:512],
                                                WT[nm][:, kc * 1024 + oc * 128: kc * 1024 + (oc + 1) * 128],
                                                xnt[:, kc * TT:(kc + 1) * TT],
                                                start=(kc == 0), stop=(kc == 7))
                        mk["proj_mm"] = kb.bump(ins, "pe", 1)

                    if nm == "g":
                        @sec("scalar")
                        def _(scalar):
                            kb.wait(scalar, "scalar", "pe", mk["proj_mm"])
                            kb.wait(scalar, "scalar", "tr", mk.get("tok_tr", 0))
                            v = kb.bump(scalar.copy(g_bf[:, oc * TT:(oc + 1) * TT], bank[:, 0:512]),
                                        "act", 1)
                            mk["bank_free"][bslot] = ("act", v)
                        return

                    @sec("scalar")
                    def _(scalar):
                        kb.wait(scalar, "scalar", "pe", mk["proj_mm"])
                        kb.wait(scalar, "scalar", "dve", mk["uext_free"])
                        v = kb.bump(scalar.copy(u_ext[:, 3:515], bank[:, 0:512]), "act", 1)
                        mk["uext_fill"] = v
                        mk["bank_free"][bslot] = ("act", v)

                    @sec("vector")
                    def _(vector):
                        kb.wait(vector, "vector", "act", mk["uext_fill"])
                        kb.wait(vector, "vector", "dve")
                        tcol = (pi * 8 + oc) * 3
                        if first_of_seq:
                            vector.memset(u_ext[:, 0:3], 0.0)
                        else:
                            vector.tensor_copy(u_ext[:, 0:3], tails[:, tcol:tcol + 3])
                        kb.fence(vector, "vector",
                                 vector.tensor_copy(tails[:, tcol:tcol + 3], u_ext[:, 512:515]),
                                 "dve")
                        wbase = pi * 32 + oc * 4
                        kb.fence(vector, "vector",
                                 vector.tensor_scalar(ca_b[:], u_ext[:, 3:515],
                                                      convp3[:, pi * 8 + oc: pi * 8 + oc + 1],
                                                      None, ALU.mult), "dve")
                        kb.fence(vector, "vector",
                                 vector.scalar_tensor_tensor(cb_b[:], u_ext[:, 2:514],
                                                             convw[:, wbase + 2: wbase + 3],
                                                             ca_b[:], ALU.mult, ALU.add), "dve")
                        kb.fence(vector, "vector",
                                 vector.scalar_tensor_tensor(ca_b[:], u_ext[:, 1:513],
                                                             convw[:, wbase + 1: wbase + 2],
                                                             cb_b[:], ALU.mult, ALU.add), "dve")
                        mk["conv_done"] = kb.bump(
                            vector.scalar_tensor_tensor(cb_b[:], u_ext[:, 0:512],
                                                        convw[:, wbase + 0: wbase + 1],
                                                        ca_b[:], ALU.mult, ALU.add), "dve", 1)
                        mk["uext_free"] = mk["conv_done"]

                    if nm == "v":
                        @sec("scalar")
                        def _(scalar):
                            kb.wait(scalar, "scalar", "dve", mk["conv_done"])
                            kb.wait(scalar, "scalar", "tr", mk.get("tok_tr", 0))
                            kb.bump(scalar.activation(v_bf[:, oc * TT:(oc + 1) * TT], cb_b[:],
                                                      AF.Silu), "act", 1)
                        return

                    dst = q_bf if nm == "q" else k_bf

                    @sec("scalar")
                    def _(scalar):
                        kb.wait(scalar, "scalar", "dve", mk["conv_done"])
                        kb.wait(scalar, "scalar", "dve", mk["qpre_free"])
                        mk["silu_done"] = kb.bump(
                            scalar.activation(qpre[:], cb_b[:], AF.Silu), "act", 1)

                    @sec("vector")
                    def _(vector):
                        kb.wait(vector, "vector", "act", mk["silu_done"])
                        kb.wait(vector, "vector", "pe", mk["sqv_free"])
                        mk["sq_done"] = kb.bump(
                            vector.tensor_tensor(sqv[:], qpre[:], qpre[:], ALU.mult), "dve", 1)

                    @sec("tensor")
                    def _(tensor):
                        kb.wait(tensor, "tensor", "dve", mk["sq_done"])
                        bs, bv = mk["bank_free"][7]
                        kb.wait(tensor, "tensor", bs, bv)
                        v = kb.bump(tensor.matmul(pb[7][0:2, 0:512], sbc["sel2_bf"][:],
                                                  sqv[:], start=True, stop=True), "pe", 1)
                        mk["rsq_mm"] = v
                        mk["sqv_free"] = v

                    @sec("scalar")
                    def _(scalar):
                        kb.wait(scalar, "scalar", "pe", mk["rsq_mm"])
                        kb.fence(scalar, "scalar",
                                 scalar.activation(sd_row[:], pb[7][0:2, 0:512], AF.Ln,
                                                   bias=CV[0:2, 1:2]), "act")
                        with nc.allow_low_precision("bf16 matmul operand (sanctioned)"):
                            ins = scalar.activation(rstd_bf[:], sd_row[:], AF.Exp, scale=-0.5)
                        v = kb.bump(ins, "act", 1)
                        mk["rstd_done"] = ("act", v)
                        mk["bank_free"][7] = ("act", v)

                    @sec("tensor")
                    def _(tensor):
                        kb.wait(tensor, "tensor", "act", mk["rstd_done"][1])
                        bs, bv = mk["bank_free"][5]
                        kb.wait(tensor, "tensor", bs, bv)
                        mk["bcast_mm"] = kb.bump(
                            tensor.matmul(pb[5][:, 0:512], sbc["selT_bf"][:], rstd_bf[:],
                                          start=True, stop=True), "pe", 1)

                    @sec("vector")
                    def _(vector):
                        kb.wait(vector, "vector", "pe", mk["bcast_mm"])
                        kb.wait(vector, "vector", "tr", mk.get("tok_tr", 0))
                        v = kb.bump(vector.tensor_tensor(dst[:, oc * TT:(oc + 1) * TT],
                                                         qpre[:], pb[5][:, 0:512], ALU.mult),
                                    "dve", 1)
                        mk["bank_free"][5] = ("dve", v)
                        mk["qpre_free"] = v

                mk["wt_ready"] = kb.val("tr")
                for tile in range(NTILE):
                    b, tl = tile // 4, tile % 4
                    g0 = tile * TT
                    first = (tl == 0)

                    @sec("sync")
                    def _(sync, g0=g0):
                        kb.wait(sync, "sync", "tr", mk["wt_ready"])
                        kb.wait(sync, "sync", "pe", mk.get("proj_mm", 0))
                        mk["xnt_ld"] = kb.bump(sync.dma_start(
                            out=xnt[:].rearrange("p (c t) -> p c t", c=8),
                            in_=hb["XNBF"].rearrange("(c p) t -> p c t", p=128)[:, :, g0:g0 + TT]),
                            "ld", 16)

                    for pi, nm in [(0, "q"), (1, "k"), (2, "v"), (3, "g")]:
                        for oc in range(8):
                            B_chunk(pi if pi < 3 else 3, nm, oc, tile, first)

                    if debug_names & {"QD", "KD", "VD", "GD"}:
                        @sec("sync")
                        def _(sync, g0=g0):
                            kb.wait(sync, "sync", "dve")
                            kb.wait(sync, "sync", "act")
                            for dn, buf in [("QD", q_bf), ("KD", k_bf), ("VD", v_bf), ("GD", g_bf)]:
                                if dn in debug_names:
                                    kb.bump(sync.dma_start(
                                        out=hb[dn].rearrange("(c p) t -> p c t", p=128)[:, :, g0:g0 + TT],
                                        in_=buf[:].rearrange("p (c t) -> p c t", c=8)), "st", 16)

                    if first:
                        @sec("tensor")
                        def _(tensor):
                            kb.wait(tensor, "tensor", "act", mk["wbf_evac"])
                            kb.wait(tensor, "tensor", "ld")
                            for c in range(8):
                                tensor.matmul(pb[0][0:64, c * 64:(c + 1) * 64],
                                              sbc["ident_f32"][0:64, 0:64],
                                              W0_sb[0:64, c * 64:(c + 1) * 64],
                                              start=(c == 0), stop=(c == 7))
                            for c in range(8):
                                ins = tensor.matmul(pb[1][0:64, c * 64:(c + 1) * 64],
                                                    sbc["ident_f32"][64:128, 64:128],
                                                    W0_sb[64:128, c * 64:(c + 1) * 64],
                                                    start=(c == 0), stop=(c == 7))
                            mk["state_mm"] = kb.bump(ins, "pe", 1)

                        @sec("scalar")
                        def _(scalar):
                            kb.wait(scalar, "scalar", "pe", mk["state_mm"])
                            scalar.copy(W_bf[0:64, :], pb[0][0:64, 0:512])
                            mk["wbf_evac"] = kb.bump(
                                scalar.copy(W_bf[64:128, :], pb[1][0:64, 0:512]), "act", 1)

                    for mc in range(4):
                        par = mc % 2
                        oa, ob = (pb[2], pb[3]) if par == 0 else (pb[4], pb[7])

                        @sec("scalar")
                        def _(scalar, mc=mc, par=par):
                            kb.wait(scalar, "scalar", "dve")
                            kb.wait(scalar, "scalar", "act")
                            kb.wait(scalar, "scalar", "pe", mk["tok_free"][par])
                            for c in range(8):
                                scalar.dma_start_transpose(
                                    ktok[par][:, c * 128:(c + 1) * 128],
                                    k_bf[:, c * TT + mc * 128: c * TT + (mc + 1) * 128]).then_inc(kb.sems["tr"], 16)
                                scalar.dma_start_transpose(
                                    vtok[par][:, c * 128:(c + 1) * 128],
                                    v_bf[:, c * TT + mc * 128: c * TT + (mc + 1) * 128]).then_inc(kb.sems["tr"], 16)
                                scalar.dma_start_transpose(
                                    gtok[par][:, c * 128:(c + 1) * 128],
                                    g_bf[:, c * TT + mc * 128: c * TT + (mc + 1) * 128]).then_inc(kb.sems["tr"], 16)
                            kb.cnt["tr"] += 16 * 24
                            mk["tok_tr"] = kb.val("tr")

                        for c in range(8):
                            @sec("tensor")
                            def _(tensor, c=c, mc=mc, par=par, oa=oa, ob=ob):
                                kb.wait(tensor, "tensor", "tr", mk["tok_tr"])
                                kb.wait(tensor, "tensor", "act", mk["wbf_evac"])
                                kb.wait(tensor, "tensor", "dve", mk.get("mask_done", 0))
                                kb.wait(tensor, "tensor", "dve", mk["o_epi"][par])
                                col = c * TT + mc * 128
                                tensor.matmul(pb[5][:, 0:128], k_bf[0:64, col:col + 128],
                                              q_bf[0:64, col:col + 128], start=True, stop=True)
                                mk["st_mm"] = kb.bump(
                                    tensor.matmul(pb[6][:, 0:128], k_bf[64:128, col:col + 128],
                                                  q_bf[64:128, col:col + 128], start=True, stop=True),
                                    "pe", 1)

                            @sec("vector")
                            def _(vector, c=c):
                                kb.wait(vector, "vector", "pe", mk["st_mm"])
                                vector.tensor_tensor(ptT[0][:], pb[5][:, 0:128],
                                                     sbc["maskTT_bf"][:], ALU.mult)
                                mk["mask_done"] = kb.bump(
                                    vector.tensor_tensor(ptT[1][:], pb[6][:, 0:128],
                                                         sbc["maskTT_bf"][:], ALU.mult), "dve", 1)

                            @sec("tensor")
                            def _(tensor, c=c, mc=mc, par=par, oa=oa, ob=ob):
                                kb.wait(tensor, "tensor", "dve", mk["mask_done"])
                                col = c * TT + mc * 128
                                ocol = c * 64
                                tensor.matmul(oa[:, ocol:ocol + 64], q_bf[0:64, col:col + 128],
                                              W_bf[0:64, ocol:ocol + 64],
                                              start=(c == 0), stop=False)
                                tensor.matmul(ob[:, ocol:ocol + 64], q_bf[64:128, col:col + 128],
                                              W_bf[64:128, ocol:ocol + 64],
                                              start=(c == 0), stop=False)
                                tensor.matmul(oa[:, ocol:ocol + 64], ptT[0][:],
                                              vtok[par][:, c * 128: c * 128 + 64],
                                              start=False, stop=(c == 7))
                                tensor.matmul(ob[:, ocol:ocol + 64], ptT[1][:],
                                              vtok[par][:, c * 128 + 64: c * 128 + 128],
                                              start=False, stop=(c == 7))
                                tensor.matmul(pb[0][0:64, ocol:ocol + 64],
                                              ktok[par][:, c * 128: c * 128 + 64],
                                              vtok[par][:, c * 128: c * 128 + 64],
                                              start=False, stop=False)
                                ins = tensor.matmul(pb[1][0:64, ocol:ocol + 64],
                                                    ktok[par][:, c * 128 + 64: c * 128 + 128],
                                                    vtok[par][:, c * 128 + 64: c * 128 + 128],
                                                    start=False, stop=False)
                                v = kb.bump(ins, "pe", 1)
                                mk["pair_mm"] = v
                                if c == 7:
                                    mk["scan_mm"] = v
                                    mk["tok_free"][par] = v

                        @sec("scalar")
                        def _(scalar, mc=mc):
                            kb.wait(scalar, "scalar", "pe", mk["scan_mm"])
                            scalar.copy(W_bf[0:64, :], pb[0][0:64, 0:512])
                            mk["wbf_evac"] = kb.bump(
                                scalar.copy(W_bf[64:128, :], pb[1][0:64, 0:512]), "act", 1)

                        @sec("vector")
                        def _(vector, mc=mc, par=par, oa=oa, ob=ob):
                            kb.wait(vector, "vector", "pe", mk["scan_mm"])
                            kb.wait(vector, "vector", "dve")
                            for h2 in range(8):
                                vector.bn_stats(mvstat[:, h2 * 6:(h2 + 1) * 6],
                                                oa[:, h2 * 64:(h2 + 1) * 64])
                            for h2 in range(8):
                                ins = vector.bn_stats(mvstat[:, 48 + h2 * 6: 48 + (h2 + 1) * 6],
                                                      ob[:, h2 * 64:(h2 + 1) * 64])
                            kb.fence(vector, "vector", ins, "dve")
                            for h in range(16):
                                base = (h % 2) * 48 + (h // 2) * 6
                                ins = vector.bn_aggr(mv[:, h * 2:(h + 1) * 2],
                                                     mvstat[:, base:base + 6])
                            mk["aggr_done"] = kb.bump(ins, "dve", 1)

                        @sec("scalar")
                        def _(scalar, mc=mc):
                            kb.wait(scalar, "scalar", "dve", mk["aggr_done"])
                            kb.fence(scalar, "scalar",
                                     scalar.activation(sdv[:], mv[:, 1:32:2], AF.Ln,
                                                       bias=CV[:, 0:1]), "act")
                            mk["osd"] = kb.bump(
                                scalar.activation(rinv[:], sdv[:], AF.Exp, scale=-0.5), "act", 1)

                        @sec("vector")
                        def _(vector, mc=mc, par=par, oa=oa, ob=ob):
                            kb.wait(vector, "vector", "act", mk["osd"])
                            for h in range(16):
                                bank = oa if h % 2 == 0 else ob
                                hcol = (h // 2) * 64
                                ins = vector.tensor_scalar(
                                    t_f[:, h * 64:(h + 1) * 64], bank[:, hcol:hcol + 64],
                                    mv[:, 2 * h:2 * h + 1], rinv[:, h:h + 1],
                                    ALU.subtract, ALU.mult)
                            v = kb.bump(ins, "dve", 1)
                            mk["o_epi"][par] = v
                            kb.wait(vector, "vector", "dve", v)
                            for h in range(16):
                                ins = vector.tensor_tensor(
                                    t2_f[:, h * 64:(h + 1) * 64],
                                    t_f[:, h * 64:(h + 1) * 64], gam_bc[:], ALU.mult)
                            kb.fence(vector, "vector", ins, "dve")
                            for h in range(16):
                                ins = vector.tensor_tensor(
                                    o_all[:, mc * 1024 + h * 64: mc * 1024 + (h + 1) * 64],
                                    t2_f[:, h * 64:(h + 1) * 64], bet_bc[:], ALU.add)
                            kb.fence(vector, "vector", ins, "dve")
                            kb.wait(vector, "vector", "tr", mk["tok_tr"])
                            v = kb.bump(vector.tensor_tensor(
                                og[:, mc * 1024:(mc + 1) * 1024],
                                o_all[:, mc * 1024:(mc + 1) * 1024], gtok[par][:], ALU.mult),
                                "dve", 1)
                            mk["og_done"][mc] = v

                        if "OD" in debug_names:
                            @sec("sync")
                            def _(sync, mc=mc, g0=g0):
                                kb.wait(sync, "sync", "dve", mk["og_done"][mc])
                                kb.bump(sync.dma_start(
                                    out=hb["OD"][g0 + mc * 128: g0 + (mc + 1) * 128, :],
                                    in_=o_all[:, mc * 1024:(mc + 1) * 1024]), "st", 16)

                        @sec("scalar")
                        def _(scalar, mc=mc):
                            kb.wait(scalar, "scalar", "dve", mk["og_done"][mc])
                            kb.wait(scalar, "scalar", "pe", mk.get("s_mm", 0))
                            for c in range(8):
                                scalar.dma_start_transpose(
                                    og_fm[:, c * TT + mc * 128: c * TT + (mc + 1) * 128],
                                    og[:, mc * 1024 + c * 128: mc * 1024 + (c + 1) * 128]).then_inc(kb.sems["tr"], 16)
                            kb.cnt["tr"] += 16 * 8
                            mk["ogfm_tr"] = kb.val("tr")

                    mk["bank_free"][2] = ("dve", mk["o_epi"][0])
                    mk["bank_free"][3] = ("dve", mk["o_epi"][0])
                    mk["bank_free"][4] = ("dve", mk["o_epi"][1])
                    mk["bank_free"][7] = ("dve", mk["o_epi"][1])

                    for oc in range(8):
                        bslot = 2 + (rot_counter[0] % 3)
                        rot_counter[0] += 1
                        bank = pb[bslot]

                        @sec("tensor")
                        def _(tensor, oc=oc, bslot=bslot, bank=bank):
                            kb.wait(tensor, "tensor", "tr", mk["ogfm_tr"])
                            kb.wait(tensor, "tensor", "dve", mk["o_epi"][0])
                            kb.wait(tensor, "tensor", "dve", mk["o_epi"][1])
                            bs, bv = mk["bank_free"][bslot] if bslot in mk["bank_free"] else ("pe", 0)
                            kb.wait(tensor, "tensor", bs, bv)
                            for kc in range(8):
                                ins = tensor.matmul(bank[:, 0:512],
                                                    WT["o"][:, kc * 1024 + oc * 128: kc * 1024 + (oc + 1) * 128],
                                                    og_fm[:, kc * TT:(kc + 1) * TT],
                                                    start=(kc == 0), stop=(kc == 7))
                            mk["s_mm"] = kb.bump(ins, "pe", 1)

                        @sec("scalar")
                        def _(scalar, oc=oc, bslot=bslot, bank=bank):
                            kb.wait(scalar, "scalar", "pe", mk["s_mm"])
                            kb.wait(scalar, "scalar", "st", mk["sst_free"][oc % 2])
                            v = kb.bump(scalar.copy(s_stage[oc % 2][:], bank[:, 0:512]), "act", 1)
                            mk["s_evac"] = v
                            mk["bank_free"][bslot] = ("act", v)

                        @sec("sync")
                        def _(sync, oc=oc, g0=g0):
                            kb.wait(sync, "sync", "act", mk["s_evac"])
                            mk["sst_free"][oc % 2] = kb.bump(sync.dma_start(
                                out=hb["S32"][oc * 128:(oc + 1) * 128, g0:g0 + TT],
                                in_=s_stage[oc % 2][:]), "st", 16)

        if "p2" in phases:
            with ExitStack() as wcx:
                wstg_f = [wcx.enter_context(nc.sbuf_tensor(f"wc_f{i}", [128, 4096], F32))
                          for i in range(2)]
                wstg_b = [wcx.enter_context(nc.sbuf_tensor(f"wc_b{i}", [128, 4096], BF16))
                          for i in range(2)]
                fqkv2 = wt["fWqkv"].rearrange("l r c -> (l r) c")
                fwo2 = wt["fWo"].rearrange("l r c -> (l r) c")
                fw12 = wt["fW1"].rearrange("l r c -> (l r) c")
                fw22 = wt["fW2"].rearrange("l r c -> (l r) c")
                jobs = []
                for L in range(depth):
                    for cch in range(NCH):
                        jobs.append((fqkv2, hb["WQKVB"], L * D + cch * 128, 128, 0, 3072))
                        jobs.append((fwo2, hb["WOB"], L * D + cch * 128, 128, 0, 1024))
                        jobs.append((fw12, hb["W1B"], L * D + cch * 128, 128, 0, 4096))
                        jobs.append((fw12, hb["W1B"], L * D + cch * 128, 128, 4096, 1364))
                    for off, ln in W2CH:
                        jobs.append((fw22, hb["W2B"], L * HALF1 + off, ln, 0, 1024))
                ldm = {}
                stm = {}
                for n in range(len(jobs) + 1):
                    @sec("gpsimd")
                    def _(gpsimd, n=n):
                        if n == 0:
                            for s in ["pe", "dve", "act", "st", "ld"]:
                                kb.wait(gpsimd, "gpsimd", s)
                        if n < len(jobs):
                            srcT, dstT, r0, rl, c0, w = jobs[n]
                            if n >= 2:
                                kb.wait(gpsimd, "gpsimd", "wt", stm[n - 2])
                            ldm[n] = kb.bump(gpsimd.dma_start(
                                out=wstg_f[n % 2][0:rl, 0:w],
                                in_=srcT[r0:r0 + rl, c0:c0 + w]), "wt", 16)
                        if n >= 1:
                            m = n - 1
                            srcT, dstT, r0, rl, c0, w = jobs[m]
                            kb.wait(gpsimd, "gpsimd", "wt", kb.val("wt"))
                            kb.fence(gpsimd, "gpsimd",
                                     gpsimd.tensor_copy(wstg_b[m % 2][0:rl, 0:w],
                                                        wstg_f[m % 2][0:rl, 0:w]), "pool")
                            stm[m] = kb.bump(gpsimd.dma_start(
                                out=dstT[r0:r0 + rl, c0:c0 + w],
                                in_=wstg_b[m % 2][0:rl, 0:w]), "wt", 16)
            precast_done = kb.val("wt")

        if "p2" in phases:
            n_sub = 2 * depth if p2_sub is None else p2_sub
            masters = [("XN32", "XNBF"), ("XA32", "XABF"), ("XB32", "XBBF")]

            with ExitStack() as octx:
                sbo = lambda name, shape, dt: octx.enter_context(nc.sbuf_tensor(name, shape, dt))
                xs_bf = sbo("p2_xs", [128, 8 * 513], BF16)
                res_f = [sbo(f"p2_res{i}", [128, 512], F32) for i in range(2)]
                xnew_f = [sbo(f"p2_xnf{i}", [128, 512], F32) for i in range(2)]
                xnew_b = [sbo(f"p2_xnb{i}", [128, 512], BF16) for i in range(2)]
                scr_f = sbo("p2_scrf", [128, 2 * 512], F32)
                scr_b = sbo("p2_scrb", [128, 2 * 512], BF16)
                rowa2 = sbo("p2_rowa2", [2, 512], F32)
                rowb2 = sbo("p2_rowb2", [2, 512], F32)
                rowbf2 = sbo("p2_rowbf2", [2, 512], BF16)
                rowa = sbo("p2_rowa", [1, 1024], F32)
                rowb = sbo("p2_rowb", [1, 1024], F32)

                pm = {"xs_ld": 0, "res_ld": [0, 0], "xnew_free": [0, 0],
                      "bank": {i: ("pe", 0) for i in range(8)},
                      "sq_free": 0, "scr_free": 0}

                def bank_wait(tensor, i):
                    s, v = pm["bank"][i]
                    kb.wait(tensor, "tensor", s, v)

                def emit_attn_sub(L, s_idx):
                    in32, inbf = masters[0] if s_idx == 0 else masters[1 + ((s_idx - 1) % 2)]
                    o32, obf = masters[1 + (s_idx % 2)]
                    with ExitStack() as actx:
                        sba = lambda name, shape, dt: actx.enter_context(
                            nc.sbuf_tensor(f"L{L}a_{name}", shape, dt))
                        Wqkv_sb = sba("wqkv", [128, 8 * 3072], BF16)
                        Wo_sb = sba("wo", [128, 8 * 1024], BF16)
                        q_b = sba("q", [128, 8 * 512], BF16)
                        k_b = [sba(f"k{i}", [128, 8 * 512], BF16) for i in range(2)]
                        v_b = sba("v", [128, 8 * 512], BF16)
                        vt_tmp = sba("vtt", [128, 1024], BF16)
                        v_tok = [sba(f"vtok{i}", [128, 4 * 1040], BF16) for i in range(2)]
                        P_sb = [sba(f"ptp{i}", [128, 384], BF16) for i in range(2)]
                        og_b = sba("og", [128, 8 * 512], BF16)
                        og_t = sba("ogt", [64, 512], BF16)
                        o_f = [sba(f"of{i}", [128, 512], F32) for i in range(2)]
                        l_f = sba("lf", [1, 1024], F32)
                        l_ln = sba("lln", [1, 1024], F32)
                        linv = sba("linv", [1, 1024], BF16)


                        @sec("sync")
                        def _(sync):
                            kb.wait(sync, "sync", "wt", precast_done)
                            kb.wait(sync, "sync", "pe")
                            kb.wait(sync, "sync", "dve")
                            kb.wait(sync, "sync", "act")
                            kb.wait(sync, "sync", "tr")
                            kb.wait(sync, "sync", "st")
                            for cch in range(NCH):
                                kb.bump(sync.dma_start(
                                    out=Wqkv_sb[:, cch * 3072:(cch + 1) * 3072],
                                    in_=hb["WQKVB"][L * D + cch * 128: L * D + (cch + 1) * 128, :]),
                                    "wt", 16)
                            for cch in range(NCH):
                                kb.bump(sync.dma_start(
                                    out=Wo_sb[:, cch * 1024:(cch + 1) * 1024],
                                    in_=hb["WOB"][L * D + cch * 128: L * D + (cch + 1) * 128, :]),
                                    "wt", 16)
                            kb.bump(sync.dma_start(out=qs_col[0:64, 0:1], in_=wt["fqs"][L, :, None]), "wt", 16)
                            kb.bump(sync.dma_start(out=qs_col[64:128, 0:1], in_=wt["fqs"][L, :, None]), "wt", 16)
                            kb.bump(sync.dma_start(out=ks_col[0:64, 0:1], in_=wt["fks"][L, :, None]), "wt", 16)
                            kb.bump(sync.dma_start(out=ks_col[64:128, 0:1], in_=wt["fks"][L, :, None]), "wt", 16)
                            for cch in range(NCH):
                                kb.bump(sync.dma_start(out=lng_cols[:, cch:cch + 1],
                                                       in_=wt["flng"][L, cch * 128:(cch + 1) * 128, None]), "wt", 16)
                                kb.bump(sync.dma_start(out=lnb_cols[:, cch:cch + 1],
                                                       in_=wt["flnb"][L, cch * 128:(cch + 1) * 128, None]), "wt", 16)
                                kb.bump(sync.dma_start(out=b2_cols[:, cch:cch + 1],
                                                       in_=wt["fb2"][L, cch * 128:(cch + 1) * 128, None]), "wt", 16)
                            for ci, (off, ln) in enumerate(W2CH):
                                kb.bump(sync.dma_start(out=b1_cols[0:ln, ci:ci + 1],
                                                       in_=wt["fb1"][L, off:off + ln, None]), "wt", 16)
                                kb.bump(sync.dma_start(out=b1_cols[0:ln, 22 + ci:23 + ci],
                                                       in_=wt["fb1"][L, HALF1 + off:HALF1 + off + ln, None]), "wt", 16)
                            pm["wt_ld"] = kb.val("wt")
                        @sec("vector")
                        def _(vector):
                            for par in range(2):
                                ins = vector.memset(
                                    v_tok[par][:].rearrange("p (b h e) -> p b h e", h=16, e=65)[:, :, :, 64:65],
                                    1.0)
                            pm["vones"] = kb.bump(ins, "dve", 1)

                        for tile in range(NTILE):
                            b, tl = tile // 4, tile % 4
                            g0 = tile * TT
                            par = tile % 2
                            tab0 = tl * TT

                            @sec("sync")
                            def _(sync, g0=g0, tl=tl):
                                kb.wait(sync, "sync", "pe", pm.get("qkv_mm", 0))
                                kb.wait(sync, "sync", "dve", pm.get("xs_dve", 0))
                                if tl == 0:
                                    pm["xs_ld"] = kb.bump(sync.dma_start(
                                        out=xs_bf[:].rearrange("p (c t) -> p c t", c=8)[:, :, 1:513],
                                        in_=hb[inbf].rearrange("(c p) t -> p c t", p=128)[:, :, g0:g0 + TT]),
                                        "ld", 16)
                                else:
                                    pm["xs_ld"] = kb.bump(sync.dma_start(
                                        out=xs_bf[:].rearrange("p (c t) -> p c t", c=8),
                                        in_=hb[inbf].rearrange("(c p) t -> p c t", p=128)[:, :, g0 - 1:g0 + TT]),
                                        "ld", 16)

                            if tl == 0:
                                @sec("vector")
                                def _(vector):
                                    kb.wait(vector, "vector", "ld", pm["xs_ld"])
                                    pm["xs_dve"] = kb.bump(vector.memset(
                                        xs_bf[:].rearrange("p (c t) -> p c t", c=8)[:, :, 0:1], 0.0),
                                        "dve", 1)

                            def xs_slice(kc):
                                if kc < 4:
                                    return xs_bf[:, kc * 513 + 1: kc * 513 + 513]
                                return xs_bf[:, kc * 513: kc * 513 + 512]

                            for j in range(24):
                                nm = "q" if j < 8 else ("k" if j < 16 else "v")
                                oc = j % 8
                                bslot = j % 3
                                bank = pb[bslot]

                                @sec("tensor")
                                def _(tensor, j=j, bslot=bslot, bank=bank, tl=tl):
                                    kb.wait(tensor, "tensor", "ld", pm["xs_ld"])
                                    if tl == 0:
                                        kb.wait(tensor, "tensor", "dve", pm["xs_dve"])
                                    kb.wait(tensor, "tensor", "wt", pm["wt_ld"])
                                    bank_wait(tensor, bslot)
                                    for kc in range(8):
                                        ins = tensor.matmul(
                                            bank[:, 0:512],
                                            Wqkv_sb[:, kc * 3072 + j * 128: kc * 3072 + (j + 1) * 128],
                                            xs_slice(kc), start=(kc == 0), stop=(kc == 7))
                                    pm["qkv_mm"] = kb.bump(ins, "pe", 1)

                                if nm == "v":
                                    @sec("scalar")
                                    def _(scalar, oc=oc, bslot=bslot, bank=bank):
                                        kb.wait(scalar, "scalar", "pe", pm["qkv_mm"])
                                        kb.wait(scalar, "scalar", "tr", pm.get("vt_tr", 0))
                                        v = kb.bump(scalar.copy(v_b[:, oc * 512:(oc + 1) * 512],
                                                                bank[:, 0:512]), "act", 1)
                                        pm["bank"][bslot] = ("act", v)
                                    continue

                                dst = q_b if nm == "q" else k_b[par]
                                tabs = (sbc["rotA"], sbc["rotB"]) if nm == "q" else (sbc["rotC"], sbc["rotD"])
                                scol = qs_col if nm == "q" else ks_col

                                @sec("scalar")
                                def _(scalar, oc=oc, bslot=bslot, bank=bank):
                                    kb.wait(scalar, "scalar", "pe", pm["qkv_mm"])
                                    kb.wait(scalar, "scalar", "dve", pm.get("rote_free", 0))
                                    v = kb.bump(scalar.copy(scr_b[:, 0:512], bank[:, 0:512]), "act", 1)
                                    pm["qevac"] = v
                                    pm["bank"][bslot] = ("act", v)

                                @sec("vector")
                                def _(vector):
                                    kb.wait(vector, "vector", "act", pm["qevac"])
                                    kb.wait(vector, "vector", "pe", pm.get("rsq_mm", 0))
                                    pm["sq_done"] = kb.bump(
                                        vector.tensor_tensor(scr_b[:, 512:1024], scr_b[:, 0:512],
                                                             scr_b[:, 0:512], ALU.mult), "dve", 1)

                                @sec("tensor")
                                def _(tensor):
                                    kb.wait(tensor, "tensor", "dve", pm["sq_done"])
                                    bank_wait(tensor, 3)
                                    pm["rsq_mm"] = kb.bump(
                                        tensor.matmul(pb[3][0:2, 0:512], sbc["sel2_bf"][:],
                                                      scr_b[:, 512:1024], start=True, stop=True), "pe", 1)

                                @sec("vector")
                                def _(vector):
                                    kb.wait(vector, "vector", "pe", pm["rsq_mm"])
                                    kb.wait(vector, "vector", "act", pm.get("nrm_r", 0))
                                    pm["nrm_mx"] = kb.bump(
                                        vector.tensor_scalar(rowa2[:], pb[3][0:2, 0:512],
                                                             1e-24, None, ALU.max), "dve", 1)
                                    pm["bank"][3] = ("dve", pm["nrm_mx"])

                                @sec("scalar")
                                def _(scalar):
                                    kb.wait(scalar, "scalar", "dve", pm["nrm_mx"])
                                    kb.wait(scalar, "scalar", "act", pm.get("nrm_r", 0))
                                    kb.wait(scalar, "scalar", "pe", pm.get("bcast_mm", 0))
                                    kb.fence(scalar, "scalar",
                                             scalar.activation(rowb2[:], rowa2[:], AF.Ln),
                                             "act")
                                    with nc.allow_low_precision("bf16 matmul operand"):
                                        ins = scalar.activation(rowbf2[:], rowb2[:], AF.Exp,
                                                                scale=-0.5)
                                    pm["nrm_r"] = kb.bump(ins, "act", 1)

                                @sec("tensor")
                                def _(tensor):
                                    kb.wait(tensor, "tensor", "act", pm["nrm_r"])
                                    bank_wait(tensor, 4)
                                    pm["bcast_mm"] = kb.bump(
                                        tensor.matmul(pb[4][:, 0:512], sbc["selT_bf"][:],
                                                      rowbf2[:], start=True, stop=True), "pe", 1)
                                    pm["bank"][4] = ("pe", pm["bcast_mm"])

                                @sec("vector")
                                def _(vector, scol=scol):
                                    kb.wait(vector, "vector", "pe", pm["bcast_mm"])
                                    pm["qn_done"] = kb.bump(vector.scalar_tensor_tensor(
                                        scr_b[:, 512:1024], scr_b[:, 0:512], scol[:],
                                        pb[4][:, 0:512], ALU.mult, ALU.mult), "dve", 1)
                                pm["bank"][4] = ("dve", pm["qn_done"])

                                @sec("tensor")
                                def _(tensor):
                                    kb.wait(tensor, "tensor", "dve", pm["qn_done"])
                                    bank_wait(tensor, 5)
                                    pm["rot_mm"] = kb.bump(
                                        tensor.matmul(pb[5][:, 0:512], sbc["R2T_bf"][:],
                                                      scr_b[:, 512:1024], start=True, stop=True), "pe", 1)

                                @sec("vector")
                                def _(vector, oc=oc, tabs=tabs, dst=dst, tab0=tab0):
                                    kb.wait(vector, "vector", "pe", pm["rot_mm"])
                                    vector.tensor_tensor(scr_b[:, 0:512], scr_b[:, 512:1024],
                                                         tabs[0][:, tab0:tab0 + 512], ALU.mult)
                                    kb.fence(vector, "vector", vector.tensor_tensor(
                                        scr_f[:, 0:512], pb[5][:, 0:512],
                                        tabs[1][:, tab0:tab0 + 512], ALU.mult), "dve")
                                    v = kb.bump(vector.tensor_tensor(
                                        dst[:, oc * 512:(oc + 1) * 512], scr_b[:, 0:512],
                                        scr_f[:, 0:512], ALU.add), "dve", 1)
                                    pm["rote_free"] = v
                                    pm["bank"][5] = ("dve", v)

                            for blk in range(4):
                                @sec("scalar")
                                def _(scalar, blk=blk, par=par):
                                    kb.wait(scalar, "scalar", "act")
                                    kb.wait(scalar, "scalar", "dve", pm.get("vtok_cp", 0))
                                    kb.wait(scalar, "scalar", "dve", pm["vones"])
                                    kb.wait(scalar, "scalar", "pe", pm.get("pv_mm", 0))
                                    for cpair in range(8):
                                        scalar.dma_start_transpose(
                                            vt_tmp[:, cpair * 128:(cpair + 1) * 128],
                                            v_b[:, cpair * 512 + blk * 128: cpair * 512 + (blk + 1) * 128]
                                        ).then_inc(kb.sems["tr"], 16)
                                    kb.cnt["tr"] += 16 * 8
                                    pm["vt_tr"] = kb.val("tr")

                                @sec("vector")
                                def _(vector, blk=blk, par=par):
                                    kb.wait(vector, "vector", "tr", pm["vt_tr"])
                                    pm["vtok_cp"] = kb.bump(vector.tensor_copy(
                                        v_tok[par][:].rearrange("p (b h e) -> p b h e", h=16, e=65)[:, blk, :, 0:64],
                                        vt_tmp[:].rearrange("p (h e) -> p h e", e=64)), "dve", 1)

                            cmin = 0 if tl == 0 else -2
                            for hp in range(8):
                                h0, h1 = 2 * hp, 2 * hp + 1
                                for c in range(cmin, 4):
                                    j0 = max(0, c)
                                    j1 = min(3, c + 2)
                                    ncols = (j1 - j0 + 1) * 128
                                    qoff = j0 * 128
                                    moff = (j0 - c) * 128
                                    if c >= 0:
                                        kb_buf, kcol = k_b[par], c * 128
                                        vt_buf, vblk = v_tok[par], c
                                    else:
                                        kb_buf, kcol = k_b[1 - par], (4 + c) * 128
                                        vt_buf, vblk = v_tok[1 - par], 4 + c

                                    @sec("tensor")
                                    def _(tensor, hp=hp, c=c, j0=j0, ncols=ncols, qoff=qoff,
                                          moff=moff, kb_buf=kb_buf, kcol=kcol):
                                        kb.wait(tensor, "tensor", "dve", pm.get("vtok_cp", 0))
                                        kb.wait(tensor, "tensor", "act", pm.get("exp_done", 0))
                                        if c == cmin:
                                            bank_wait(tensor, 5)
                                            kb.wait(tensor, "tensor", "dve", pm.get("ogTT", 0))
                                        tensor.matmul(pb[5][:, 0:ncols],
                                                      kb_buf[0:64, hp * 512 + kcol: hp * 512 + kcol + 128],
                                                      q_b[0:64, hp * 512 + qoff: hp * 512 + qoff + ncols],
                                                      start=True, stop=False)
                                        tensor.matmul(pb[6][:, 0:ncols],
                                                      kb_buf[64:128, hp * 512 + kcol: hp * 512 + kcol + 128],
                                                      q_b[64:128, hp * 512 + qoff: hp * 512 + qoff + ncols],
                                                      start=True, stop=False)
                                        tensor.matmul(pb[5][:, 0:ncols], sbc["negI_bf"][:],
                                                      sbc["maskU_bf"][:, moff:moff + ncols],
                                                      start=False, stop=True)
                                        pm["st_mm"] = kb.bump(
                                            tensor.matmul(pb[6][:, 0:ncols], sbc["negI_bf"][:],
                                                          sbc["maskU_bf"][:, moff:moff + ncols],
                                                          start=False, stop=True), "pe", 1)

                                    @sec("scalar")
                                    def _(scalar, ncols=ncols):
                                        kb.wait(scalar, "scalar", "pe", pm["st_mm"])
                                        kb.wait(scalar, "scalar", "pe", pm.get("pv_mm", 0))
                                        scalar.activation(P_sb[0][:, 0:ncols], pb[5][:, 0:ncols],
                                                          AF.Exp, bias=CV[:, 2:3], scale=8.0)
                                        pm["exp_done"] = kb.bump(
                                            scalar.activation(P_sb[1][:, 0:ncols], pb[6][:, 0:ncols],
                                                              AF.Exp, bias=CV[:, 2:3], scale=8.0), "act", 1)

                                    @sec("tensor")
                                    def _(tensor, hp=hp, c=c, j0=j0, j1=j1, ncols=ncols,
                                          qoff=qoff, vt_buf=vt_buf, vblk=vblk, h0=h0, h1=h1):
                                        kb.wait(tensor, "tensor", "act", pm["exp_done"])
                                        if c == cmin:
                                            kb.wait(tensor, "tensor", "act", pm.get("oevac", 0))
                                            kb.wait(tensor, "tensor", "act", pm.get("levac", 0))
                                            bank_wait(tensor, 3)
                                            bank_wait(tensor, 4)
                                        tensor.matmul(pb[3][0:65, qoff:qoff + ncols],
                                                      vt_buf[:, vblk * 1040 + h0 * 65: vblk * 1040 + h0 * 65 + 65],
                                                      P_sb[0][:, 0:ncols],
                                                      start=(c == cmin), stop=(c == 3))
                                        pm["pv_mm"] = kb.bump(tensor.matmul(
                                            pb[4][0:65, qoff:qoff + ncols],
                                            vt_buf[:, vblk * 1040 + h1 * 65: vblk * 1040 + h1 * 65 + 65],
                                            P_sb[1][:, 0:ncols],
                                            start=(c == cmin), stop=(c == 3)), "pe", 1)

                                @sec("scalar")
                                def _(scalar, hp=hp):
                                    kb.wait(scalar, "scalar", "pe", pm["pv_mm"])
                                    kb.wait(scalar, "scalar", "dve", pm.get("ogTT", 0))
                                    scalar.copy(l_f[0:1, 0:512], pb[3][64:65, 0:512])
                                    pm["levac"] = kb.bump(scalar.copy(l_f[0:1, 512:1024],
                                                                      pb[4][64:65, 0:512]), "act", 1)
                                    scalar.copy(o_f[0][0:64, :], pb[3][0:64, 0:512])
                                    pm["oevac"] = kb.bump(scalar.copy(o_f[1][0:64, :], pb[4][0:64, 0:512]),
                                                          "act", 1)

                                @sec("scalar")
                                def _(scalar, hp=hp):
                                    kb.wait(scalar, "scalar", "act", pm["levac"])
                                    kb.wait(scalar, "scalar", "act", pm.get("linv", 0))
                                    kb.wait(scalar, "scalar", "pe", pm.get("lb_mm", 0))
                                    kb.fence(scalar, "scalar",
                                             scalar.activation(l_ln[:], l_f[:], AF.Ln), "act")
                                    with nc.allow_low_precision("bf16 matmul operand"):
                                        ins = scalar.activation(linv[:], l_ln[:], AF.Exp,
                                                                scale=-1.0)
                                    pm["linv"] = kb.bump(ins, "act", 1)

                                @sec("tensor")
                                def _(tensor, hp=hp):
                                    kb.wait(tensor, "tensor", "act", pm["linv"])
                                    bank_wait(tensor, 0)
                                    bank_wait(tensor, 1)
                                    tensor.matmul(pb[0][0:64, 0:512], sbc["ones_row_bf"][0:1, 0:64],
                                                  linv[0:1, 0:512], start=True, stop=True)
                                    pm["lb_mm"] = kb.bump(
                                        tensor.matmul(pb[1][0:64, 0:512], sbc["ones_row_bf"][0:1, 0:64],
                                                      linv[0:1, 512:1024], start=True, stop=True),
                                        "pe", 1)

                                @sec("vector")
                                def _(vector, hp=hp):
                                    kb.wait(vector, "vector", "pe", pm["lb_mm"])
                                    kb.wait(vector, "vector", "act", pm["oevac"])
                                    kb.wait(vector, "vector", "act", pm.get("og_cp", 0))
                                    vector.tensor_tensor(og_b[0:64, hp * 512:(hp + 1) * 512],
                                                         o_f[0][0:64, :], pb[0][0:64, 0:512], ALU.mult)
                                    v = kb.bump(vector.tensor_tensor(
                                        og_t[0:64, :],
                                        o_f[1][0:64, :], pb[1][0:64, 0:512], ALU.mult), "dve", 1)
                                    pm["ogTT"] = v
                                    pm["bank"][0] = ("dve", v)
                                    pm["bank"][1] = ("dve", v)

                                @sec("scalar")
                                def _(scalar, hp=hp):
                                    kb.wait(scalar, "scalar", "dve", pm["ogTT"])
                                    v = kb.bump(scalar.copy(og_b[64:128, hp * 512:(hp + 1) * 512],
                                                            og_t[0:64, :]), "act", 1)
                                    pm["og_cp"] = v
                                    if hp == 7:
                                        pm["og_all_act"] = v

                            if debug_names & {"Q2D", "K2D", "V2D", "OGD"}:
                                @sec("sync")
                                def _(sync, g0=g0, par=par):
                                    kb.wait(sync, "sync", "dve")
                                    kb.wait(sync, "sync", "act")
                                    for dn, buf in [("Q2D", q_b), ("K2D", k_b[par]),
                                                    ("V2D", v_b), ("OGD", og_b)]:
                                        if dn in debug_names:
                                            kb.bump(sync.dma_start(
                                                out=hb[dn].rearrange("(c p) t -> p c t", p=128)[:, :, g0:g0 + TT],
                                                in_=buf[:].rearrange("p (c t) -> p c t", c=8)), "st", 16)

                            for oc in range(8):
                                bslot = 1 + (oc % 2)
                                bank = pb[bslot]

                                @sec("sync")
                                def _(sync, oc=oc, g0=g0):
                                    kb.wait(sync, "sync", "dve", pm["xnew_free"][oc % 2])
                                    pm["res_ld"][oc % 2] = kb.bump(sync.dma_start(
                                        out=res_f[oc % 2][:],
                                        in_=hb[in32][oc * 128:(oc + 1) * 128, g0:g0 + TT]), "ld", 16)

                                @sec("tensor")
                                def _(tensor, oc=oc, bslot=bslot, bank=bank):
                                    kb.wait(tensor, "tensor", "dve", pm["ogTT"])
                                    kb.wait(tensor, "tensor", "act", pm["og_all_act"])
                                    bank_wait(tensor, bslot)
                                    for kc in range(8):
                                        ins = tensor.matmul(
                                            bank[:, 0:512],
                                            Wo_sb[:, kc * 1024 + oc * 128: kc * 1024 + (oc + 1) * 128],
                                            og_b[:, kc * 512:(kc + 1) * 512],
                                            start=(kc == 0), stop=(kc == 7))
                                    pm["wo_mm"] = kb.bump(ins, "pe", 1)

                                @sec("vector")
                                def _(vector, oc=oc, bslot=bslot, bank=bank):
                                    kb.wait(vector, "vector", "pe", pm["wo_mm"])
                                    kb.wait(vector, "vector", "ld", pm["res_ld"][oc % 2])
                                    kb.wait(vector, "vector", "st", pm.get("xst_%d" % (oc % 2), 0))
                                    v = kb.bump(vector.tensor_tensor(
                                        xnew_f[oc % 2][:], bank[:, 0:512], res_f[oc % 2][:], ALU.add),
                                        "dve", 1)
                                    pm["bank"][bslot] = ("dve", v)
                                    kb.wait(vector, "vector", "dve", v)
                                    pm["xnew_dve"] = kb.bump(vector.tensor_copy(
                                        xnew_b[oc % 2][:], xnew_f[oc % 2][:]), "dve", 1)
                                    pm["xnew_free"][oc % 2] = pm["xnew_dve"]

                                @sec("sync")
                                def _(sync, oc=oc, g0=g0):
                                    kb.wait(sync, "sync", "dve", pm["xnew_dve"])
                                    kb.bump(sync.dma_start(
                                        out=hb[o32][oc * 128:(oc + 1) * 128, g0:g0 + TT],
                                        in_=xnew_f[oc % 2][:]), "st", 16)
                                    pm["xst_%d" % (oc % 2)] = kb.bump(sync.dma_start(
                                        out=hb[obf][oc * 128:(oc + 1) * 128, g0:g0 + TT],
                                        in_=xnew_b[oc % 2][:]), "st", 16)

                def emit_ffn_sub(L, s_idx):
                    in32, inbf = masters[1 + ((s_idx - 1) % 2)]
                    o32, obf = masters[1 + (s_idx % 2)]
                    with ExitStack() as fctx:
                        sbf = lambda name, shape, dt: fctx.enter_context(
                            nc.sbuf_tensor(f"L{L}f_{name}", shape, dt))
                        W1_sb = sbf("w1", [128, 8 * W1O], BF16)
                        w2p = [sbf(f"w2p{i}", [128, 1024], BF16) for i in range(2)]
                        xn_b = sbf("xn", [128, 8 * 512], BF16)
                        h_sb = sbf("h", [128, 22 * 512], BF16)

                        @sec("sync")
                        def _(sync):
                            kb.wait(sync, "sync", "wt", precast_done)
                            kb.wait(sync, "sync", "pe")
                            kb.wait(sync, "sync", "dve")
                            kb.wait(sync, "sync", "act")
                            kb.wait(sync, "sync", "tr")
                            kb.wait(sync, "sync", "st")
                            for cch in range(NCH):
                                kb.bump(sync.dma_start(
                                    out=W1_sb[:, cch * W1O:(cch + 1) * W1O],
                                    in_=hb["W1B"][L * D + cch * 128: L * D + (cch + 1) * 128, :]),
                                    "wt", 16)
                            pm["w1_ld"] = kb.val("wt")

                        for tile in range(NTILE):
                            b, tl = tile // 4, tile % 4
                            g0 = tile * TT

                            @sec("sync")
                            def _(sync, g0=g0, tl=tl):
                                kb.wait(sync, "sync", "pe", pm.get("ln_mm", 0))
                                kb.wait(sync, "sync", "dve", pm.get("xn_done", 0))
                                kb.wait(sync, "sync", "act", pm.get("sq_act", 0))
                                if tl == 0:
                                    pm["xs_ld"] = kb.bump(sync.dma_start(
                                        out=xs_bf[:].rearrange("p (c t) -> p c t", c=8)[:, :, 1:513],
                                        in_=hb[inbf].rearrange("(c p) t -> p c t", p=128)[:, :, g0:g0 + TT]),
                                        "ld", 16)
                                else:
                                    pm["xs_ld"] = kb.bump(sync.dma_start(
                                        out=xs_bf[:].rearrange("p (c t) -> p c t", c=8),
                                        in_=hb[inbf].rearrange("(c p) t -> p c t", p=128)[:, :, g0 - 1:g0 + TT]),
                                        "ld", 16)

                            if tl == 0:
                                @sec("vector")
                                def _(vector):
                                    kb.wait(vector, "vector", "ld", pm["xs_ld"])
                                    pm["xs_dve"] = kb.bump(vector.memset(
                                        xs_bf[:].rearrange("p (c t) -> p c t", c=8)[:, :, 0:1], 0.0),
                                        "dve", 1)

                            def xs_slice(kc):
                                if kc < 4:
                                    return xs_bf[:, kc * 513 + 1: kc * 513 + 513]
                                return xs_bf[:, kc * 513: kc * 513 + 512]

                            for kc in range(8):
                                @sec("scalar")
                                def _(scalar, kc=kc, tl=tl):
                                    kb.wait(scalar, "scalar", "ld", pm["xs_ld"])
                                    kb.wait(scalar, "scalar", "pe", pm.get("ssq_mm", 0))
                                    pm["sq_act"] = kb.bump(scalar.activation(
                                        scr_b[:, 0:512], xs_slice(kc), AF.Square), "act", 1)

                                @sec("tensor")
                                def _(tensor, kc=kc, tl=tl):
                                    if tl == 0:
                                        kb.wait(tensor, "tensor", "dve", pm.get("xs_dve", 0))
                                    kb.wait(tensor, "tensor", "act", pm["sq_act"])
                                    if kc == 0:
                                        bank_wait(tensor, 3)
                                        bank_wait(tensor, 4)
                                    tensor.matmul(pb[3][0:1, 0:512], sbc["ones_col_bf"][:],
                                                  xs_slice(kc), start=(kc == 0), stop=(kc == 7))
                                    pm["ssq_mm"] = kb.bump(tensor.matmul(
                                        pb[4][0:1, 0:512], sbc["ones_col_bf"][:],
                                        scr_b[:, 0:512], start=(kc == 0), stop=(kc == 7)), "pe", 1)
                            pm["ln_mm"] = pm["ssq_mm"]

                            @sec("vector")
                            def _(vector):
                                kb.wait(vector, "vector", "pe", pm["ssq_mm"])
                                kb.wait(vector, "vector", "pe", pm.get("ln_bc", 0))
                                kb.wait(vector, "vector", "act", pm.get("ln_r", 0))
                                kb.wait(vector, "vector", "dve")
                                vector.tensor_scalar(rowa[0:1, 0:512], pb[3][0:1, 0:512],
                                                     1.0 / 1024.0, None, ALU.mult)
                                kb.fence(vector, "vector", vector.tensor_scalar(
                                    rowa[0:1, 512:1024], pb[4][0:1, 0:512], 1.0 / 1024.0, None,
                                    ALU.mult), "dve")
                                kb.fence(vector, "vector", vector.tensor_tensor(
                                    rowb[0:1, 0:512], rowa[0:1, 0:512], rowa[0:1, 0:512],
                                    ALU.mult), "dve")
                                pm["var_done"] = kb.bump(vector.tensor_tensor(
                                    rowb[0:1, 512:1024], rowa[0:1, 512:1024], rowb[0:1, 0:512],
                                    ALU.subtract), "dve", 1)

                            @sec("scalar")
                            def _(scalar):
                                kb.wait(scalar, "scalar", "dve", pm["var_done"])
                                kb.fence(scalar, "scalar", scalar.activation(
                                    rowb[0:1, 0:512], rowb[0:1, 512:1024], AF.Ln,
                                    bias=CV[0:1, 0:1]), "act")
                                pm["ln_r"] = kb.bump(scalar.activation(
                                    rowb[0:1, 512:1024], rowb[0:1, 0:512], AF.Exp,
                                    scale=-0.5), "act", 1)

                            @sec("tensor")
                            def _(tensor):
                                kb.wait(tensor, "tensor", "act", pm["ln_r"])
                                tensor.matmul(pb[3][:, 0:512], sbc["ones_row_f32"][0:1, :],
                                              rowa[0:1, 0:512], start=True, stop=True)
                                pm["ln_bc"] = kb.bump(tensor.matmul(
                                    pb[4][:, 0:512], sbc["ones_row_f32"][0:1, :],
                                    rowb[0:1, 512:1024], start=True, stop=True), "pe", 1)

                            for kc in range(8):
                                @sec("vector")
                                def _(vector, kc=kc):
                                    kb.wait(vector, "vector", "pe", pm["ln_bc"])
                                    kb.fence(vector, "vector", vector.tensor_tensor(
                                        scr_f[:, 0:512], xs_slice(kc), pb[3][:, 0:512],
                                        ALU.subtract), "dve")
                                    kb.fence(vector, "vector", vector.scalar_tensor_tensor(
                                        scr_f[:, 512:1024], scr_f[:, 0:512],
                                        lng_cols[:, kc:kc + 1], pb[4][:, 0:512],
                                        ALU.mult, ALU.mult), "dve")
                                    v = kb.bump(vector.tensor_scalar(
                                        xn_b[:, kc * 512:(kc + 1) * 512], scr_f[:, 512:1024],
                                        lnb_cols[:, kc:kc + 1], None, ALU.add), "dve", 1)
                                    pm["xn_done"] = v
                            pm["bank"][3] = ("dve", pm["xn_done"])
                            pm["bank"][4] = ("dve", pm["xn_done"])

                            for ci, (off, ln) in enumerate(W2CH):
                                pa = pb[(2 * ci) % 4]
                                pg = pb[(2 * ci) % 4 + 1]

                                @sec("tensor")
                                def _(tensor, ci=ci, off=off, ln=ln, pa=pa, pg=pg):
                                    kb.wait(tensor, "tensor", "dve", pm["xn_done"])
                                    kb.wait(tensor, "tensor", "wt", pm["w1_ld"])
                                    bank_wait(tensor, (2 * ci) % 4)
                                    bank_wait(tensor, (2 * ci) % 4 + 1)
                                    for kc in range(8):
                                        tensor.matmul(pa[0:ln, 0:512],
                                                      W1_sb[:, kc * W1O + off: kc * W1O + off + ln],
                                                      xn_b[:, kc * 512:(kc + 1) * 512],
                                                      start=(kc == 0), stop=(kc == 7))
                                    for kc in range(8):
                                        ins = tensor.matmul(
                                            pg[0:ln, 0:512],
                                            W1_sb[:, kc * W1O + HALF1 + off: kc * W1O + HALF1 + off + ln],
                                            xn_b[:, kc * 512:(kc + 1) * 512],
                                            start=(kc == 0), stop=(kc == 7))
                                    pm["w1_mm"] = kb.bump(ins, "pe", 1)

                                @sec("scalar")
                                def _(scalar, ci=ci, ln=ln, pg=pg):
                                    kb.wait(scalar, "scalar", "pe", pm["w1_mm"])
                                    kb.wait(scalar, "scalar", "dve", pm.get("h_done", 0))
                                    pm["gelu"] = kb.bump(scalar.activation(
                                        scr_b[0:ln, 0:512], pg[0:ln, 0:512], AF.Gelu,
                                        bias=b1_cols[0:ln, 22 + ci:23 + ci]), "act", 1)

                                @sec("vector")
                                def _(vector, ci=ci, ln=ln, pa=pa, pg=pg):
                                    kb.wait(vector, "vector", "act", pm["gelu"])
                                    kb.fence(vector, "vector", vector.tensor_scalar(
                                        scr_b[0:ln, 512:1024], pa[0:ln, 0:512],
                                        b1_cols[0:ln, ci:ci + 1], None, ALU.add), "dve")
                                    v = kb.bump(vector.tensor_tensor(
                                        h_sb[0:ln, ci * 512:(ci + 1) * 512],
                                        scr_b[0:ln, 512:1024], scr_b[0:ln, 0:512], ALU.mult),
                                        "dve", 1)
                                    pm["h_done"] = v
                                    pm["bank"][(2 * ci) % 4] = ("dve", v)
                                    pm["bank"][(2 * ci) % 4 + 1] = ("dve", v)

                            for ci, (off, ln) in enumerate(W2CH):
                                @sec("sync")
                                def _(sync, ci=ci, off=off, ln=ln, L=L):
                                    kb.wait(sync, "sync", "pe", pm.get("w2_mm_%d" % (ci % 2), 0))
                                    pm["w2_ld"] = kb.bump(sync.dma_start(
                                        out=w2p[ci % 2][0:ln, :],
                                        in_=hb["W2B"][L * HALF1 + off: L * HALF1 + off + ln, :]),
                                        "wt", 16)

                                @sec("tensor")
                                def _(tensor, ci=ci, off=off, ln=ln):
                                    kb.wait(tensor, "tensor", "wt", pm["w2_ld"])
                                    kb.wait(tensor, "tensor", "dve", pm["h_done"])
                                    if ci == 0:
                                        for i2 in range(8):
                                            bank_wait(tensor, i2)
                                    for oc in range(8):
                                        ins = tensor.matmul(
                                            pb[oc][:, 0:512],
                                            w2p[ci % 2][0:ln, oc * 128:(oc + 1) * 128],
                                            h_sb[0:ln, ci * 512:(ci + 1) * 512],
                                            start=(ci == 0), stop=(ci == 21))
                                    pm["w2_mm_%d" % (ci % 2)] = kb.bump(ins, "pe", 1)
                                    if ci == 21:
                                        pm["w2_mm"] = pm["w2_mm_%d" % (ci % 2)]

                            for oc in range(8):
                                @sec("sync")
                                def _(sync, oc=oc, g0=g0):
                                    kb.wait(sync, "sync", "dve", pm["xnew_free"][oc % 2])
                                    pm["res_ld"][oc % 2] = kb.bump(sync.dma_start(
                                        out=res_f[oc % 2][:],
                                        in_=hb[in32][oc * 128:(oc + 1) * 128, g0:g0 + TT]), "ld", 16)

                                @sec("vector")
                                def _(vector, oc=oc):
                                    kb.wait(vector, "vector", "pe", pm["w2_mm"])
                                    kb.wait(vector, "vector", "ld", pm["res_ld"][oc % 2])
                                    kb.wait(vector, "vector", "st", pm.get("xst_%d" % (oc % 2), 0))
                                    v = kb.bump(vector.scalar_tensor_tensor(
                                        xnew_f[oc % 2][:], res_f[oc % 2][:],
                                        b2_cols[:, oc:oc + 1], pb[oc][:, 0:512],
                                        ALU.add, ALU.add), "dve", 1)
                                    pm["bank"][oc] = ("dve", v)
                                    kb.wait(vector, "vector", "dve", v)
                                    pm["xnew_dve"] = kb.bump(vector.tensor_copy(
                                        xnew_b[oc % 2][:], xnew_f[oc % 2][:]), "dve", 1)
                                    pm["xnew_free"][oc % 2] = pm["xnew_dve"]

                                @sec("sync")
                                def _(sync, oc=oc, g0=g0):
                                    kb.wait(sync, "sync", "dve", pm["xnew_dve"])
                                    kb.bump(sync.dma_start(
                                        out=hb[o32][oc * 128:(oc + 1) * 128, g0:g0 + TT],
                                        in_=xnew_f[oc % 2][:]), "st", 16)
                                    pm["xst_%d" % (oc % 2)] = kb.bump(sync.dma_start(
                                        out=hb[obf][oc * 128:(oc + 1) * 128, g0:g0 + TT],
                                        in_=xnew_b[oc % 2][:]), "st", 16)

                for s_idx in range(n_sub):
                    L = s_idx // 2
                    if s_idx % 2 == 0:
                        emit_attn_sub(L, s_idx)
                    else:
                        emit_ffn_sub(L, s_idx)

        if "p3" in phases:
            fin32 = masters[1 + ((2 * depth - 1) % 2)][0]
            with ExitStack() as pctx:
                sb3 = lambda name, shape, dt: pctx.enter_context(nc.sbuf_tensor(name, shape, dt))
                xf_t = sb3("p3_xf", [128, 8 * 512], F32)
                s_t = sb3("p3_s", [128, 8 * 512], F32)
                x_o = sb3("p3_xo", [128, 4 * 1024], F32)
                out_sb = sb3("p3_out", [128, 4 * 1024], F32)
                sg_t = sb3("p3_sg", [128, 8 * 512], F32)
                fm = {"ld": 0, "sg": 0, "tp": [0] * 8, "tt": [0] * 8, "st": 0}

                for tile in range(NTILE):
                    g0 = tile * TT

                    @sec("sync")
                    def _(sync, g0=g0, tile=tile):
                        if tile == 0:
                            for s in ["pe", "dve", "act", "pool", "ld", "wt", "tr"]:
                                kb.wait(sync, "sync", s)
                        else:
                            kb.wait(sync, "sync", "dve", fm["tt_last"])
                            kb.wait(sync, "sync", "act", fm["silu"])
                        kb.wait(sync, "sync", "st")
                        kb.bump(sync.dma_start(
                            out=xf_t[:].rearrange("p (c t) -> p c t", c=8),
                            in_=hb[fin32].rearrange("(c p) t -> p c t", p=128)[:, :, g0:g0 + TT]),
                            "ld", 16)
                        kb.bump(sync.dma_start(
                            out=s_t[:].rearrange("p (c t) -> p c t", c=8),
                            in_=hb["S32"].rearrange("(c p) t -> p c t", p=128)[:, :, g0:g0 + TT]),
                            "ld", 16)
                        fm["ld"] = kb.bump(sync.dma_start(
                            out=x_o[:].rearrange("p (q d) -> p q d", q=4),
                            in_=xin.rearrange("(q p) d -> p q d", p=128)[:, 4 * tile:4 * tile + 4, :]),
                            "ld", 16)

                    @sec("scalar")
                    def _(scalar, tile=tile):
                        kb.wait(scalar, "scalar", "ld", fm["ld"])
                        kb.wait(scalar, "scalar", "dve", fm.get("sg_done", 0))
                        for cch in range(8):
                            ins = scalar.activation(sg_t[:, cch * 512:(cch + 1) * 512],
                                                    xf_t[:, cch * 512:(cch + 1) * 512], AF.Silu)
                        fm["silu"] = kb.bump(ins, "act", 1)

                    @sec("vector")
                    def _(vector, tile=tile):
                        kb.wait(vector, "vector", "act", fm["silu"])
                        kb.wait(vector, "vector", "pe", fm.get("tp_last", 0))
                        for cch in range(8):
                            ins = vector.tensor_tensor(xf_t[:, cch * 512:(cch + 1) * 512],
                                                       sg_t[:, cch * 512:(cch + 1) * 512],
                                                       s_t[:, cch * 512:(cch + 1) * 512], ALU.mult)
                        fm["sg_done"] = kb.bump(ins, "dve", 1)

                    for qb in range(4):
                        @sec("tensor")
                        def _(tensor, qb=qb, tile=tile):
                            kb.wait(tensor, "tensor", "dve", fm["sg_done"])
                            for cch in range(8):
                                kb.wait(tensor, "tensor", "dve", fm["tt"][cch])
                                ins = tensor.transpose(pb[cch][:, 0:128],
                                                       xf_t[:, cch * 512 + qb * 128: cch * 512 + (qb + 1) * 128],
                                                       sbc["ident_f32"][:])
                                fm["tp"][cch] = kb.bump(ins, "pe", 1)
                            fm["tp_last"] = fm["tp"][7]

                        @sec("vector")
                        def _(vector, qb=qb, tile=tile):
                            kb.wait(vector, "vector", "st", fm["st"])
                            for cch in range(8):
                                kb.wait(vector, "vector", "pe", fm["tp"][cch])
                                ins = vector.tensor_tensor(
                                    out_sb[:, qb * 1024 + cch * 128: qb * 1024 + (cch + 1) * 128],
                                    pb[cch][:, 0:128],
                                    x_o[:, qb * 1024 + cch * 128: qb * 1024 + (cch + 1) * 128],
                                    ALU.add)
                                fm["tt"][cch] = kb.bump(ins, "dve", 1)
                            fm["tt_last"] = fm["tt"][7]

                    @sec("sync")
                    def _(sync, g0=g0, tile=tile):
                        kb.wait(sync, "sync", "dve", fm["tt_last"])
                        fm["st"] = kb.bump(sync.dma_start(
                            out=out_ap.rearrange("(q p) d -> p q d", p=128)[:, 4 * tile:4 * tile + 4, :],
                            in_=out_sb[:].rearrange("p (q d) -> p q d", q=4)), "st", 16)

        @sec("sync")
        def _(sync):
            kb.wait(sync, "sync", "st")
            kb.wait(sync, "sync", "wt")
            kb.wait(sync, "sync", "act")
            kb.wait(sync, "sync", "dve")
            kb.wait(sync, "sync", "pe")
            for dn in debug_names:
                kb.bump(sync.dma_start(out=dbg[dn], in_=hb[dn]), "st", 16)
            kb.wait(sync, "sync", "st")

    return nc, consts


def make_in_maps(inputs, consts):
    x = np.ascontiguousarray(inputs["L0_attn_hidden_states"])
    in_maps = []
    ncores = x.shape[0] // 2
    for core in range(ncores):
        m = {"x": np.ascontiguousarray(x[2 * core:2 * core + 2].reshape(NTOK, D))}
        m.update({
            "aWq": inputs["L0_attn_Wq"], "aWk": inputs["L0_attn_Wk"], "aWv": inputs["L0_attn_Wv"],
            "aWg": inputs["L0_attn_Wg"], "aWo": inputs["L0_attn_Wo"],
            "acq": inputs["L0_attn_conv_q"], "ack": inputs["L0_attn_conv_k"],
            "acv": inputs["L0_attn_conv_v"],
            "aW0": np.ascontiguousarray(inputs["L0_attn_W_init"].reshape(16 * DK, DK)),
            "agam": inputs["L0_attn_ln_gamma"], "abet": inputs["L0_attn_ln_beta"],
            "fWqkv": inputs["L0_ffn_Wqkv"], "fqs": inputs["L0_ffn_q_scale"],
            "fks": inputs["L0_ffn_k_scale"], "fWo": inputs["L0_ffn_Wo"],
            "flng": inputs["L0_ffn_ln_g"], "flnb": inputs["L0_ffn_ln_b"],
            "fW1": inputs["L0_ffn_W1"], "fb1": inputs["L0_ffn_b1"],
            "fW2": inputs["L0_ffn_W2"], "fb2": inputs["L0_ffn_b2"],
        })
        for nm in CONST_SPECS:
            m["c_" + nm] = consts[nm]
        in_maps.append(m)
    return in_maps


def kernel(**inputs):
    nc, consts = build_program()
    in_maps = make_in_maps(inputs, consts)
    res = run_bass_kernel_spmd(nc, in_maps, core_ids=list(range(8)))
    outs = [r["out"].reshape(2, SEQ, D) for r in res.results]
    return np.concatenate(outs, axis=0)
